# version 19 (final)
# baseline (speedup 1.0000x reference)
import contextlib
import numpy as np
import concourse.bass as bass
import concourse.mybir as mybir
from concourse.bass_utils import run_bass_kernel_spmd

F32 = mybir.dt.float32
BF16 = mybir.dt.bfloat16
I32 = mybir.dt.int32
AF = mybir.ActivationFunctionType
ALU = mybir.AluOpType
AX = mybir.AxisListType

B = 2
T = 16384
D = 1024
NCORE = 8
QPB = 4
TSH = T // QPB
NB = 8
TB = TSH // NB
DC = 8
FC = 16
H = 8
E = 8
FF = 2048
EPS = 1e-5

STAGE = 3
DEBUG = False


class SemC:
    def __init__(self, es, nc, name):
        self.h = es.enter_context(nc.semaphore(name))
        self.v = 0

    def inc(self, inst, n=1):
        inst.then_inc(self.h, n)
        self.v += n
        return (self, self.v)


def wt(eng, *ms):
    best = {}
    for m in ms:
        if m is None:
            continue
        if isinstance(m, (list, tuple)) and m and isinstance(m[0], (list, tuple)):
            for mm in m:
                if mm is None:
                    continue
                sem, val = mm
                if val > best.get(id(sem), (None, 0))[1]:
                    best[id(sem)] = (sem, val)
            continue
        sem, val = m
        if val > best.get(id(sem), (None, 0))[1]:
            best[id(sem)] = (sem, val)
    for sem, val in best.values():
        eng.wait_ge(sem.h, val)


class Ctx:
    pass


class _MockInst:
    __slots__ = ("calls",)

    def __init__(self):
        self.calls = []

    def then_inc(self, sem, n=1):
        self.calls.append((sem, n))
        return self


class _EngProxy:
    def __init__(self, rec):
        self._rec = rec

    def __getattr__(self, attr):
        rec = self._rec

        def call(*a, **k):
            mi = _MockInst()
            rec.append((attr, a, k, mi))
            return mi
        return call


class DeferredBlock:
    ENGS = ("sync", "scalar", "vector", "tensor", "gpsimd")

    def __init__(self):
        self.rec = {e: [] for e in self.ENGS}

    def _mk(self, name):
        def run(fn):
            fn(_EngProxy(self.rec[name]))
        return run

    def __getattr__(self, name):
        if name in self.ENGS:
            return self._mk(name)
        raise AttributeError(name)

    def finalize(self, blk):
        for e in self.ENGS:
            ops = self.rec[e]
            if not ops:
                continue

            def _replay(eng, ops=ops):
                for attr, a, k, mi in ops:
                    inst = getattr(eng, attr)(*a, **k)
                    for sem, n in mi.calls:
                        inst.then_inc(sem, n)
            getattr(blk, e)(_replay)


def build_program():
    nc = bass.Bass("TRN2", target_bir_lowering=False)
    c = Ctx()
    c.nc = nc
    c.st = {}

    c.x0h = nc.dram_tensor("x0", [D, TSH], F32, kind="ExternalInput")
    c.outh = nc.dram_tensor("y", [D, TSH], F32, kind="ExternalOutput")
    c.lrh = nc.dram_tensor("lastrow", [128, DC], F32, kind="ExternalInput")
    c.metah = nc.dram_tensor("meta", [1, 32], F32, kind="ExternalInput")
    c.W1h = nc.dram_tensor("W1", [E, D, FF], F32, kind="ExternalInput")
    c.W2h = nc.dram_tensor("W2", [E, FF, D], F32, kind="ExternalInput")
    c.Woh = nc.dram_tensor("Wo", [D, D], F32, kind="ExternalInput")
    c.wp1h = nc.dram_tensor("Wp1", [D, 128], F32, kind="ExternalInput")
    c.wp2h = nc.dram_tensor("Wp2", [128, E], F32, kind="ExternalInput")
    c.bp1h = nc.dram_tensor("bp1", [128, 1], F32, kind="ExternalInput")
    c.bp2h = nc.dram_tensor("bp2", [E, 1], F32, kind="ExternalInput")
    c.b1cmh = nc.dram_tensor("b1cm", [128, E * FC], F32, kind="ExternalInput")
    c.b2cmh = nc.dram_tensor("b2cm", [128, E * DC], F32, kind="ExternalInput")
    c.bocmh = nc.dram_tensor("bocm", [128, DC], F32, kind="ExternalInput")
    c.winh = nc.dram_tensor("w_in", [H, 128, 128], F32, kind="ExternalInput")
    c.wah = nc.dram_tensor("w_a", [H, 128, 128], F32, kind="ExternalInput")
    c.bincmh = nc.dram_tensor("bincm", [128, H], F32, kind="ExternalInput")
    c.bacmh = nc.dram_tensor("bacm", [128, H], F32, kind="ExternalInput")
    c.aparh = nc.dram_tensor("aparcm", [128, H], F32, kind="ExternalInput")
    c.selrh = nc.dram_tensor("selrows", [2, 256], F32, kind="ExternalInput")
    c.x1sh = nc.dram_tensor("x1s", [D, TSH], F32)
    c.x2sh = nc.dram_tensor("x2s", [D, TSH], F32)
    c.ln2dh = nc.dram_tensor("ln2d", [2, TSH], F32)
    c.ccin = [nc.dram_tensor(f"ccin{h}", [128, 2], F32) for h in range(H)]
    c.ccout = [nc.dram_tensor(f"ccout{h}", [NCORE * 128, 2], F32, addr_space="Shared")
               for h in range(H)]
    c.cc2in = nc.dram_tensor("cc2in", [1, 8], F32)
    c.cc2out = nc.dram_tensor("cc2out", [NCORE, 8], F32, addr_space="Shared")
    if DEBUG:
        c.dbgh = nc.dram_tensor("dbg", [128, 64], F32, kind="ExternalOutput")
        c.dbg2h = nc.dram_tensor("dbg2", [2, TSH], F32, kind="ExternalOutput")

    es = contextlib.ExitStack()
    with es:
        c.LD = SemC(es, nc, "LD")
        c.LX = [SemC(es, nc, "LX0"), SemC(es, nc, "LX1")]
        c.GD = SemC(es, nc, "GD")
        c.GD2 = SemC(es, nc, "GD2")
        c.GDC = SemC(es, nc, "GDC")
        c.GDS = [SemC(es, nc, f"GDS{i}") for i in range(2 * DC)]
        c.GDX = SemC(es, nc, "GDX")
        c.GDL = SemC(es, nc, "GDL")
        c.GDA = SemC(es, nc, "GDA")
        c.CC = SemC(es, nc, "CC")
        c.PE = SemC(es, nc, "PE")
        c.DV = SemC(es, nc, "DV")
        c.SC = SemC(es, nc, "SC")
        c.PO = SemC(es, nc, "PO")

        def sb(name, shape, dt=F32):
            return es.enter_context(nc.sbuf_tensor(name, shape, dt))

        c.wob = sb("wob", [128, DC * D], BF16)
        c.winb = sb("winb", [128, H * 128], BF16)
        c.wab = sb("wab", [128, H * 128], BF16)
        c.wp2f = sb("wp2f", [128, E], F32)
        c.bp1c = sb("bp1c", [128, 1], F32)
        c.bp2c = sb("bp2c", [E, 1], F32)
        c.selmk = sb("selmk", [E, 1], F32)
        c.b1cm = sb("b1cm_s", [128, E * FC], F32)
        c.b2cm = sb("b2cm_s", [128, E * DC], F32)
        c.bocm = sb("bocm_s", [128, DC], F32)
        c.bincm = sb("bincm_s", [128, H], F32)
        c.bacm = sb("bacm_s", [128, H], F32)
        c.aparcm = sb("aparcm_s", [128, H], F32)
        c.sp4n = sb("sp4n", [128, H], F32)
        c.binh = sb("binh", [128, H], F32)
        c.bahh = sb("bahh", [128, H], F32)
        c.selr = sb("selr", [2, 256], F32)
        c.lr_t = sb("lr_t", [128, DC], F32)
        c.meta_t = sb("meta_t", [1, 32], F32)
        c.metaB = sb("metaB", [128, 32], F32)
        c.omask = sb("omask", [128, 8], F32)
        c.omr0 = sb("omr0", [128, 1], F32)
        c.ones128f = sb("ones128f", [128, 1], F32)
        c.ones128b = sb("ones128b", [128, 1], BF16)
        c.ones1x128 = sb("ones1x128", [1, 128], F32)
        c.io7mi = sb("io7mi", [1, 8], F32)
        c.ioW1 = sb("ioW1", [128, DC], F32)
        c.ioW2 = sb("ioW2", [128, FC], F32)
        c.rt_sq = sb("rt_sq", [128, DC], F32)
        c.rt_pk = sb("rt_pk", [128, 2], F32)
        c.rt_row = sb("rt_row", [1, 2], F32)
        c.rt_m = sb("rt_m", [1, 1], F32)
        c.rt_q = sb("rt_q", [1, 1], F32)
        c.rt_s = sb("rt_s", [1, 1], F32)
        c.rt_u = sb("rt_u", [128, DC], F32)
        c.rt_zg = sb("rt_zg", [128, 1], F32)
        c.rt_lg = sb("rt_lg", [E, 1], F32)
        c.rt_lgrow = sb("rt_lgrow", [1, E], F32)
        c.rt_mx = sb("rt_mx", [1, 1], F32)
        c.rt_oh = sb("rt_oh", [1, E], F32)
        c.rt_ag = sb("rt_ag", [E, E], F32)
        c.rt_selB = sb("rt_selB", [128, 1], F32)
        c.rt_ohB = sb("rt_ohB", [128, E], F32)
        c.idxW1f = sb("idxW1f", [128, DC], F32)
        c.idxW2f = sb("idxW2f", [128, FC], F32)
        c.idxW1 = sb("idxW1", [128, DC], I32)
        c.idxW2 = sb("idxW2", [128, FC], I32)
        c.b1sel = sb("b1sel", [128, FC], F32)
        c.b2sel = sb("b2sel", [128, DC], F32)
        c.b2selb = sb("b2selb", [128, DC], BF16)
        c.bo15 = sb("bo15", [128, DC], F32)
        c.stats1 = sb("stats1", [128, TSH], F32)
        c.rB_t = sb("rB_t", [128, TB], F32)
        c.mB_t = sb("mB_t", [128, TB], F32)
        c.lnrow = sb("lnrow", [128, TB], F32)
        c.summ = sb("summ", [128, 2], F32)
        c.ag_t = sb("ag_t", [128, 16], F32)
        c.peff = sb("peff", [128, 8], F32)
        c.yeff = sb("yeff", [128, 8], F32)
        c.carry = sb("carry", [128, 1], F32)
        c.x2last = sb("x2last", [128, DC], F32)
        c.sumlog = sb("sumlog", [128, 1], F32)
        if DEBUG:
            c.dbg_t = sb("dbg_t", [128, 64], F32)
        c.fin0 = sb("fin0", [128, 32], F32)
        c.fin1 = sb("fin1", [128, 32], F32)
        c.fin2 = sb("fin2", [128, 32], F32)
        c.fin3 = sb("fin3", [128, 32], F32)
        c.ccol = sb("ccol", [128, 1], F32)
        c.ccol2 = sb("ccol2", [128, 1], F32)
        c.xb = [sb("xb0", [128, TSH], F32), sb("xb1", [128, TSH], F32)]
        c.tfp = [sb("tf0", [128, TSH], F32), sb("tf1", [128, TSH], F32)]
        c.ub = sb("ub", [128, TSH], BF16)

        c.pBig = es.enter_context(nc.psum_tensor("pBig", [128, 4 * TB], F32))
        c.pBig2 = es.enter_context(nc.psum_tensor("pBig2", [128, 2 * TB], F32))
        c.pH = [c.pBig[:, 0:TB], c.pBig[:, TB:2 * TB]]
        c.pR = [c.pBig[:, 2 * TB:3 * TB], c.pBig[:, 3 * TB:4 * TB]]
        c.pO = [c.pBig2[:, 0:TB], c.pBig2[:, TB:2 * TB]]
        c.pS = es.enter_context(nc.psum_tensor("pS", [128, TB], F32))
        c.pX = es.enter_context(nc.psum_tensor("pX", [128, TB], F32))

        with contextlib.ExitStack() as es1:
            with contextlib.ExitStack() as esr:
                c.wp1f = esr.enter_context(nc.sbuf_tensor("wp1f_a", [128, DC * 128], F32))
            c.W1b = es1.enter_context(nc.sbuf_tensor("W1b_a", [128, DC * FF], BF16))
            c.W2b = es1.enter_context(nc.sbuf_tensor("W2b_a", [128, FC * D], BF16))
            c.hb = es1.enter_context(nc.sbuf_tensor("hb_a", [128, FC * TB], BF16))
            c.rb = es1.enter_context(nc.sbuf_tensor("rb_a", [128, DC * TB], BF16))
            c.sqs = es1.enter_context(nc.sbuf_tensor("sqs_a", [128, TB], BF16))
            dfb = DeferredBlock()
            _emit_startup(c, dfb)
            _emit_router(c, dfb, which=1)
            _emit_gathers(c, dfb, which=1)
            _emit_ffn(c, dfb, call=1, srch=c.x0h,
                      dsth=(c.x1sh if STAGE >= 2 else c.outh), do_ln2=(STAGE >= 2))
            with nc.Block() as blk:
                dfb.finalize(blk)

        if STAGE >= 2:
            with contextlib.ExitStack() as es2:
                c.stats2 = es2.enter_context(nc.sbuf_tensor("stats2", [128, TSH], F32))
                c.a_t = es2.enter_context(nc.sbuf_tensor("a_t", [128, TSH], F32))
                c.nx_t = es2.enter_context(nc.sbuf_tensor("nx_t", [128, TSH], F32))
                c.ml_t = es2.enter_context(nc.sbuf_tensor("ml_t", [128, TSH], F32))
                c.P_t = es2.enter_context(nc.sbuf_tensor("P_t", [128, TSH], F32))
                c.y_t = c.tfp[1]
                dfb = DeferredBlock()
                _emit_attn(c, dfb, final=(STAGE == 2))
                with nc.Block() as blk:
                    dfb.finalize(blk)

        if STAGE >= 3:
            with contextlib.ExitStack() as es3:
                with contextlib.ExitStack() as esr3:
                    c.wp1f = esr3.enter_context(nc.sbuf_tensor("wp1f_b", [128, DC * 128], F32))
                c.W1b = es3.enter_context(nc.sbuf_tensor("W1b_b", [128, DC * FF], BF16))
                c.W2b = es3.enter_context(nc.sbuf_tensor("W2b_b", [128, FC * D], BF16))
                c.hb = es3.enter_context(nc.sbuf_tensor("hb_b", [128, FC * TB], BF16))
                c.rb = es3.enter_context(nc.sbuf_tensor("rb_b", [128, DC * TB], BF16))
                c.sqs = es3.enter_context(nc.sbuf_tensor("sqs_b", [128, TB], BF16))
                dfb = DeferredBlock()
                _load_wp1(c, dfb)
                _emit_router(c, dfb, which=2)
                _emit_gathers(c, dfb, which=2)
                _emit_ffn(c, dfb, call=2, srch=c.x2sh, dsth=c.outh, do_ln2=False,
                          skip_stats=True)
                with nc.Block() as blk:
                    dfb.finalize(blk)

        finals = [(s_, s_.v) for s_ in [c.GD, c.GD2, c.GDC, c.GDL, c.GDA, c.GDX, c.LD,
                                        c.LX[0], c.LX[1]] + c.GDS]
        with nc.Block() as blk:
            def _fin(g):
                wt(g, *finals)
            blk.gpsimd(_fin)

    return nc


def _load_wp1(c, blk):
    def _s(s):
        c.st["wp1"] = c.LD.inc(s.dma_start(
            c.wp1f[:, :], bass.AP(c.wp1h, 0, [[128, 128], [128 * 128, DC], [1, 128]])), 16)
    blk.sync(_s)


def _emit_startup(c, blk):
    st = c.st
    _load_wp1(c, blk)

    def _sync(s):
        st["lr"] = c.LD.inc(s.dma_start(c.lr_t[:, :], c.lrh.ap()[:, :]), 16)
        st["meta"] = c.LD.inc(s.dma_start(c.meta_t[:, :], c.metah.ap()[:, :]), 16)
        c.LD.inc(s.dma_start(c.wp2f[:, :], c.wp2h.ap()[:, :]), 16)
        c.LD.inc(s.dma_start(c.bp1c[:, :], c.bp1h.ap()[:, :]), 16)
        c.LD.inc(s.dma_start(c.bp2c[:, :], c.bp2h.ap()[:, :]), 16)
        c.LD.inc(s.dma_start(c.selmk[:, :], bass.AP(c.metah, 9, [[1, E], [1, 1]])), 16)
        c.LD.inc(s.dma_start(c.b1cm[:, :], c.b1cmh.ap()[:, :]), 16)
        c.LD.inc(s.dma_start(c.b2cm[:, :], c.b2cmh.ap()[:, :]), 16)
        c.LD.inc(s.dma_start(c.bocm[:, :], c.bocmh.ap()[:, :]), 16)
        c.LD.inc(s.dma_start(c.bincm[:, :], c.bincmh.ap()[:, :]), 16)
        c.LD.inc(s.dma_start(c.bacm[:, :], c.bacmh.ap()[:, :]), 16)
        c.LD.inc(s.dma_start(c.selr[:, :], c.selrh.ap()[:, :]), 16)
        st["consts"] = c.LD.inc(s.dma_start(c.aparcm[:, :], c.aparh.ap()[:, :]), 16)
        st["wp1"] = st["consts"]
    blk.sync(_sync)

    def _gp(g):
        c.GDC.inc(g.dma_start(c.winb[:, :],
                              bass.AP(c.winh, 0, [[128, 128], [128 * 128, H], [1, 128]])), 16)
        st["wab"] = c.GDC.inc(g.dma_start(
            c.wab[:, :], bass.AP(c.wah, 0, [[128, 128], [128 * 128, H], [1, 128]])), 16)
        m = None
        for k in range(DC):
            m = c.GDC.inc(g.dma_start(c.wob[:, k * D:(k + 1) * D],
                                      bass.AP(c.Woh, k * 128 * D, [[D, 128], [1, D]])), 16)
        st["wo"] = m
        st["wab"] = m
        st["io1"] = c.PO.inc(g.iota(c.ioW1[:, :], [[128, DC]], channel_multiplier=1,
                                    allow_small_or_imprecise_dtypes=True))
        st["io2"] = c.PO.inc(g.iota(c.ioW2[:, :], [[128, FC]], channel_multiplier=1,
                                    allow_small_or_imprecise_dtypes=True))
        st["io7"] = c.PO.inc(g.iota(c.io7mi[:, :], [[-1, 8]], base=7, channel_multiplier=0,
                                    allow_small_or_imprecise_dtypes=True))
    blk.gpsimd(_gp)

    def _v(v):
        c.DV.inc(v.memset(c.ones128f[:, :], 1.0))
        c.DV.inc(v.memset(c.ones128b[:, :], 1.0))
        st["ones"] = c.DV.inc(v.memset(c.ones1x128[:, :], 1.0))
    blk.vector(_v)

    def _sc(s):
        wt(s, st["consts"], st["ones"])
        m1 = c.SC.inc(s.activation(c.sp4n[:, :], c.aparcm[:, :], AF.Exp))
        wt(s, m1)
        st["sp"] = c.SC.inc(s.activation(c.sp4n[:, :], c.sp4n[:, :], AF.Ln,
                                         bias=c.ones128f[:, :1]))
    blk.scalar(_sc)

    def _v2(v):
        wt(v, st["sp"])
        st["sp4n"] = c.DV.inc(v.tensor_scalar_mul(c.sp4n[:, :], c.sp4n[:, :], -4.0))
        wt(v, st["consts"])
        st["binh"] = c.DV.inc(v.tensor_scalar_mul(c.binh[:, :], c.bincm[:, :], 0.5))
        st["bahh"] = c.DV.inc(v.tensor_scalar_mul(c.bahh[:, :], c.bacm[:, :], 0.5))
    blk.vector(_v2)

    def _pe(t):
        wt(t, st["consts"], st["ones"])
        st["metaB_mm"] = c.PE.inc(t.matmul(c.pX[:, 0:32], c.ones1x128[:, :], c.meta_t[:, :],
                                           start=True, stop=True))
    blk.tensor(_pe)

    def _v3(v):
        wt(v, st["metaB_mm"])
        m = c.DV.inc(v.tensor_copy(c.metaB[:, :], c.pX[:, 0:32]))
        st["metaB"] = m
        wt(v, m)
        st["omask"] = c.DV.inc(v.tensor_scalar(c.omask[:, :], c.metaB[:, 1:9], -1.0, 1.0,
                                               ALU.mult, ALU.add))
        st["omr0"] = c.DV.inc(v.tensor_scalar(c.omr0[:, :], c.metaB[:, 0:1], -1.0, 1.0,
                                              ALU.mult, ALU.add))
    blk.vector(_v3)
    st["pX_free"] = st["metaB"]


def _emit_router(c, blk, which):
    st = c.st
    src = c.lr_t if which == 1 else c.x2last
    ms_src = st["consts"] if which == 1 else st["x2last_ms"]

    def _sc1(s):
        wt(s, ms_src)
        st["r1"] = c.SC.inc(s.activation(c.rt_sq[:, :], src[:, :], AF.Square))
    blk.scalar(_sc1)

    def _v1(v):
        wt(v, st["r1"])
        c.DV.inc(v.tensor_reduce(c.rt_pk[:, 1:2], c.rt_sq[:, :], AX.X, ALU.add))
        wt(v, ms_src)
        st["r2"] = c.DV.inc(v.tensor_reduce(c.rt_pk[:, 0:1], src[:, :], AX.X, ALU.add))
    blk.vector(_v1)

    def _pe1(t):
        wt(t, st["r2"], st["pX_free"])
        st["r3"] = c.PE.inc(t.matmul(c.pX[0:1, 0:2], c.ones128f[:, :], c.rt_pk[:, :],
                                     start=True, stop=True))
    blk.tensor(_pe1)

    def _v2(v):
        wt(v, st["r3"])
        m = c.DV.inc(v.tensor_copy(c.rt_row[:, :], c.pX[0:1, 0:2]))
        wt(v, m)
        m = c.DV.inc(v.tensor_scalar_mul(c.rt_m[:, :], c.rt_row[:, 0:1], 1.0 / D))
        m2 = c.DV.inc(v.tensor_scalar_mul(c.rt_q[:, :], c.rt_row[:, 1:2], 1.0 / D))
        wt(v, m, m2)
        m = c.DV.inc(v.tensor_mul(c.rt_s[:, :], c.rt_m[:, :], c.rt_m[:, :]))
        wt(v, m)
        m = c.DV.inc(v.tensor_sub(c.rt_q[:, :], c.rt_q[:, :], c.rt_s[:, :]))
        wt(v, m)
        m = c.DV.inc(v.tensor_scalar_add(c.rt_q[:, :], c.rt_q[:, :], EPS))
        wt(v, m)
        st["r4"] = c.DV.inc(v.reciprocal(c.rt_q[:, :], c.rt_q[:, :]))
    blk.vector(_v2)

    def _sc2(s):
        wt(s, st["r4"])
        st["r5"] = c.SC.inc(s.activation(c.rt_s[:, :], c.rt_q[:, :], AF.Sqrt))
    blk.scalar(_sc2)

    def _pe2(t):
        wt(t, st["r5"])
        c.PE.inc(t.matmul(c.pX[:, 2:3], c.ones1x128[:, :], c.rt_s[:, :], start=True, stop=True))
        st["r6"] = c.PE.inc(t.matmul(c.pX[:, 3:4], c.ones1x128[:, :], c.rt_m[:, :],
                                     start=True, stop=True))
    blk.tensor(_pe2)

    def _v3(v):
        wt(v, st["r6"])
        m = c.DV.inc(v.tensor_copy(c.ccol[:, :], c.pX[:, 2:3]))
        m2 = c.DV.inc(v.tensor_copy(c.ccol2[:, :], c.pX[:, 3:4]))
        wt(v, m, m2, ms_src)
        m = c.DV.inc(v.tensor_scalar(c.rt_u[:, :], src[:, :], c.ccol2[:, 0:1], None,
                                     ALU.subtract))
        wt(v, m)
        st["r7"] = c.DV.inc(v.tensor_scalar(c.rt_u[:, :], c.rt_u[:, :], c.ccol[:, 0:1], None,
                                            ALU.mult))
    blk.vector(_v3)

    def _pe3(t):
        wt(t, st["r7"], st["wp1"])
        m = None
        for k in range(DC):
            m = c.PE.inc(t.matmul(c.pX[:, 4:5], c.wp1f[:, 128 * k:128 * (k + 1)],
                                  c.rt_u[:, k:k + 1], start=(k == 0), stop=(k == DC - 1)))
        st["r8"] = m
    blk.tensor(_pe3)

    def _sc3(s):
        wt(s, st["r8"])
        st["r9"] = c.SC.inc(s.activation(c.rt_zg[:, :], c.pX[:, 4:5], AF.Gelu,
                                         bias=c.bp1c[:, :1]))
    blk.scalar(_sc3)

    def _pe4(t):
        wt(t, st["r9"])
        st["r10"] = c.PE.inc(t.matmul(c.pX[0:E, 5:6], c.wp2f[:, :], c.rt_zg[:, :],
                                      start=True, stop=True))
    blk.tensor(_pe4)

    def _v4(v):
        wt(v, st["r10"], st["consts"])
        st["r11"] = c.DV.inc(v.tensor_add(c.rt_lg[:, :], c.pX[0:E, 5:6], c.bp2c[:, :]))
    blk.vector(_v4)

    def _gp1(g):
        wt(g, st["r11"])
        st["r12"] = c.GDA.inc(g.dma_start(c.rt_lgrow[:, :], c.rt_lg[:, :]), 16)
    blk.gpsimd(_gp1)

    def _v5(v):
        wt(v, st["r12"])
        m = c.DV.inc(v.reduce_max(c.rt_mx[:, :], c.rt_lgrow[:, :], axis=AX.X))
        wt(v, m)
        st["r13"] = c.DV.inc(v.tensor_scalar(c.rt_oh[:, :], c.rt_lgrow[:, :],
                                             c.rt_mx[:, 0:1], None, ALU.is_ge))
    blk.vector(_v5)

    if which == 2:
        def _gp2(g):
            wt(g, st["r13"])
            m = c.GDA.inc(g.dma_start(c.cc2in.ap()[:, :], c.rt_oh[:, :]), 16)
            wt(g, m)
            m = c.CC.inc(g.collective_compute(
                "AllGather", ALU.bypass, replica_groups=[list(range(NCORE))],
                ins=[c.cc2in.ap().opt()], outs=[c.cc2out.ap().opt()]))
            wt(g, m)
            st["r14"] = c.GDA.inc(g.dma_start(c.rt_ag[:, :], c.cc2out.ap()[:, :]), 16)
        blk.gpsimd(_gp2)

        def _v6(v):
            wt(v, st["r14"], st["consts"])
            st["r15"] = c.DV.inc(v.tensor_scalar(c.rt_ag[:, :], c.rt_ag[:, :],
                                                 c.selmk[:, 0:1], None, ALU.mult))
        blk.vector(_v6)

        def _pe5(t):
            wt(t, st["r15"])
            st["r16"] = c.PE.inc(t.matmul(c.pX[0:1, 8:16], c.ones128f[0:E, :], c.rt_ag[:, :],
                                          start=True, stop=True))
        blk.tensor(_pe5)

        def _v7(v):
            wt(v, st["r16"])
            st["r13"] = c.DV.inc(v.tensor_copy(c.rt_oh[:, :], c.pX[0:1, 8:16]))
        blk.vector(_v7)

    def _v8(v):
        wt(v, st["r13"], st["io7"])
        m = c.DV.inc(v.tensor_mul(c.rt_lgrow[:, :], c.rt_oh[:, :], c.io7mi[:, :]))
        wt(v, m)
        m = c.DV.inc(v.reduce_max(c.rt_mx[:, :], c.rt_lgrow[:, :], axis=AX.X))
        wt(v, m)
        st["r17"] = c.DV.inc(v.tensor_scalar(c.rt_mx[:, :], c.rt_mx[:, :], -1.0, 7.0,
                                             ALU.mult, ALU.add))
    blk.vector(_v8)

    def _pe6(t):
        wt(t, st["r17"])
        c.PE.inc(t.matmul(c.pX[:, 6:7], c.ones1x128[:, :], c.rt_mx[:, :], start=True, stop=True))
        st["r18"] = c.PE.inc(t.matmul(c.pX[:, 16:24], c.ones1x128[:, :], c.rt_oh[:, :],
                                      start=True, stop=True))
    blk.tensor(_pe6)

    def _v9(v):
        wt(v, st["r18"])
        m1 = c.DV.inc(v.tensor_copy(c.rt_selB[:, :], c.pX[:, 6:7]))
        m2 = c.DV.inc(v.tensor_copy(c.rt_ohB[:, :], c.pX[:, 16:24]))
        wt(v, m1, m2, st["io1"], st["io2"])
        m = c.DV.inc(v.scalar_tensor_tensor(c.idxW1f[:, :], c.rt_selB[:, 0:1].broadcast_to([128, DC]),
                                            float(D), c.ioW1[:, :], ALU.mult, ALU.add))
        m2 = c.DV.inc(v.scalar_tensor_tensor(c.idxW2f[:, :], c.rt_selB[:, 0:1].broadcast_to([128, FC]),
                                             float(FF), c.ioW2[:, :], ALU.mult, ALU.add))
        wt(v, m, m2)
        st["idx1"] = c.DV.inc(v.tensor_copy(c.idxW1[:, :], c.idxW1f[:, :]))
        st["idx2"] = c.DV.inc(v.tensor_copy(c.idxW2[:, :], c.idxW2f[:, :]))
        wt(v, st["consts"])
        m = c.DV.inc(v.tensor_scalar(c.b1sel[:, :], c.b1cm[:, 0:FC], c.rt_ohB[:, 0:1],
                                     None, ALU.mult))
        for e in range(1, E):
            wt(v, m)
            m = c.DV.inc(v.scalar_tensor_tensor(c.b1sel[:, :], c.b1cm[:, e * FC:(e + 1) * FC],
                                                c.rt_ohB[:, e:e + 1], c.b1sel[:, :],
                                                ALU.mult, ALU.add))
        st["b1sel"] = m
        m = c.DV.inc(v.tensor_scalar(c.b2sel[:, :], c.b2cm[:, 0:DC], c.rt_ohB[:, 0:1],
                                     None, ALU.mult))
        for e in range(1, E):
            wt(v, m)
            m = c.DV.inc(v.scalar_tensor_tensor(c.b2sel[:, :], c.b2cm[:, e * DC:(e + 1) * DC],
                                                c.rt_ohB[:, e:e + 1], c.b2sel[:, :],
                                                ALU.mult, ALU.add))
        wt(v, m)
        st["b2selb"] = c.DV.inc(v.tensor_copy(c.b2selb[:, :], c.b2sel[:, :]))
    blk.vector(_v9)

    def _pe7(t):
        wt(t, st["b2selb"], st["wo"])
        m = None
        for m3 in range(DC):
            for k in range(DC):
                m = c.PE.inc(t.matmul(c.pX[:, 32 + m3:33 + m3],
                                      c.wob[:, k * D + 128 * m3:k * D + 128 * (m3 + 1)],
                                      c.b2selb[:, k:k + 1], start=(k == 0), stop=(k == DC - 1)))
        st["bo_mm"] = m
    blk.tensor(_pe7)

    def _v10(v):
        wt(v, st["bo_mm"], st["consts"])
        m = c.DV.inc(v.tensor_add(c.bo15[:, :], c.pX[:, 32:32 + DC], c.bocm[:, :]))
        wt(v, m)
        st["bo15"] = c.DV.inc(v.tensor_scalar_mul(c.bo15[:, :], c.bo15[:, :], 0.15))
        st["pX_free"] = st["bo15"]
    blk.vector(_v10)


def _emit_gathers(c, blk, which):
    st = c.st

    def _gp(g):
        m = None
        for ci in range(DC + FC):
            isW1 = ci < DC
            cc = ci if isW1 else ci - DC
            width = FF if isW1 else D
            idxt = c.idxW1 if isW1 else c.idxW2
            idx_ms = st["idx1"] if isW1 else st["idx2"]
            srch = c.W1h if isW1 else c.W2h
            dst = c.W1b[:, cc * FF:(cc + 1) * FF] if isW1 else c.W2b[:, cc * D:(cc + 1) * D]
            wt(g, idx_ms)
            sem = c.GD if isW1 else c.GD2
            m = sem.inc(g.indirect_dma_start(
                dst, None, srch.ap()[:, :, :],
                bass.IndirectOffsetOnAxis(idxt[:, cc:cc + 1], axis=1)), 16)
            if ci == DC - 1:
                st["W1_all"] = m
        st["W2_all"] = m
    blk.gpsimd(_gp)


def _emit_ffn(c, blk, call, srch, dsth, do_ln2, skip_stats=False):
    st = c.st
    src_ready = st.get("gd_after_attn") if call == 2 else None
    xb, ub, hb, rb, sqs = c.xb, c.ub, c.hb, c.rb, c.sqs
    stats = c.stats1
    pH, pR, pO, pS, pX = c.pH, c.pR, c.pO, c.pS, c.pX

    def src_ap(n):
        return bass.AP(srch, n * TB, [[TSH, 128], [128 * TSH, DC], [1, TB]])

    def dst_ap(n, ch):
        return bass.AP(dsth, n * TB + ch * 128 * TSH, [[TSH, 128], [1, TB]])

    xb_last = [None, None]
    ub_last = None
    pS_prev = None
    for n in range(NB) if not skip_stats else []:
        slot = n % 2

        def _sync(s, n=n, slot=slot, war=xb_last[slot]):
            wt(s, war, src_ready)
            st[f"px{call}_{n}"] = c.LX[slot].inc(s.dma_start(
                xb[slot][:, :].rearrange("p (c t) -> p c t", c=DC), src_ap(n)), 16)
        blk.sync(_sync)

        def _sc(s, n=n, slot=slot, prev_ub=ub_last):
            wt(s, st[f"px{call}_{n}"], prev_ub)
            st[f"psq{call}_{n}"] = c.SC.inc(s.activation(ub[:, :], xb[slot][:, :], AF.Square))
        blk.scalar(_sc)

        def _vc(v, n=n, slot=slot, prev_ub=ub_last):
            wt(v, st[f"px{call}_{n}"], prev_ub)
            st[f"pxc{call}_{n}"] = c.DV.inc(v.tensor_copy(hb[:, 0:TSH], xb[slot][:, :]))
        blk.vector(_vc)

        def _pe(t, n=n, slot=slot, prev=pS_prev):
            wt(t, st[f"psq{call}_{n}"], st[f"pxc{call}_{n}"], prev)
            for k in range(DC):
                t.matmul(pS[0:1, :], c.ones128b[:, :], hb[:, k * TB:(k + 1) * TB],
                         start=(k == 0), stop=(k == DC - 1))
            mm = None
            for k in range(DC):
                mm = t.matmul(pS[32:33, :], c.ones128b[:, :], ub[:, k * TB:(k + 1) * TB],
                              start=(k == 0), stop=(k == DC - 1), tile_position=(0, 32))
            st[f"pmm{call}_{n}"] = c.PE.inc(mm)
        blk.tensor(_pe)
        xb_last[slot] = st[f"pmm{call}_{n}"]
        ub_last = st[f"pmm{call}_{n}"]

        def _v(v, n=n):
            wt(v, st[f"pmm{call}_{n}"])
            c.DV.inc(v.tensor_copy(stats[0:1, n * TB:(n + 1) * TB], pS[0:1, :]))
            st[f"pst{call}_{n}"] = c.DV.inc(
                v.tensor_copy(stats[32:33, n * TB:(n + 1) * TB], pS[32:33, :]))
        blk.vector(_v)
        pS_prev = st[f"pst{call}_{n}"]

    if not skip_stats:
        _emit_finalize(c, blk, stats, stats, st[f"pst{call}_{NB-1}"], f"f{call}")
        st[f"rmv{call}"] = st[f"f{call}_done"]
    else:
        st[f"rmv{call}"] = st["rmv_ffn2"]

    bc_free = st["pX_free"]
    lnst = [None]

    def emit_load(n):
        slot = n % 2

        def _sync(s, n=n, slot=slot, war=xb_last[slot]):
            wt(s, war, src_ready)
            st[f"mx{call}_{n}"] = c.LX[slot].inc(s.dma_start(
                xb[slot][:, :].rearrange("p (c t) -> p c t", c=DC), src_ap(n)), 16)
        blk.sync(_sync)

    def emit_bcast(n):
        nonlocal bc_free

        def _pe_bc(t, n=n, war=bc_free):
            wt(t, st[f"rmv{call}"], war)
            st[f"bc{call}_{n}"] = c.PE.inc(t.matmul(
                pX[:, :], c.selr[:, 128:256], stats[0:2, n * TB:(n + 1) * TB],
                start=True, stop=True))
        blk.tensor(_pe_bc)

        def _sc_bc(s, n=n):
            wt(s, st[f"bc{call}_{n}"], st.get(f"ub{call}_{n-1}"))
            st[f"bcE{call}_{n}"] = c.SC.inc(s.copy(c.rB_t[:, :], pX[:, :]))
        blk.scalar(_sc_bc)

        def _pe_bc2(t, n=n):
            wt(t, st[f"bcE{call}_{n}"])
            st[f"bc2{call}_{n}"] = c.PE.inc(t.matmul(
                pX[:, :], c.selr[:, 0:128], stats[0:2, n * TB:(n + 1) * TB],
                start=True, stop=True))
        blk.tensor(_pe_bc2)

        def _sc_bc2(s, n=n):
            wt(s, st[f"bc2{call}_{n}"], st.get(f"ub{call}_{n-1}"))
            st[f"bcE2{call}_{n}"] = c.SC.inc(s.copy(c.mB_t[:, :], pX[:, :]))
        blk.scalar(_sc_bc2)
        bc_free = st[f"bcE2{call}_{n}"]

    def emit_norm(n):
        slot = n % 2
        tf = c.tfp[slot]

        def _v_u(v, n=n, slot=slot, tf=tf):
            wt(v, st[f"mx{call}_{n}"], st[f"bcE{call}_{n}"], st[f"bcE2{call}_{n}"],
               st.get(f"tfwar{call}_{n-2}"), st.get(f"h{call}_{n-1}_last"),
               (st.get(f"pmm{call}_{NB-1}") if not skip_stats else None) if n <= 1 else None)
            m = None
            for ci in range(DC):
                m = c.DV.inc(v.tensor_mul(tf[:, ci * TB:(ci + 1) * TB],
                                          xb[slot][:, ci * TB:(ci + 1) * TB], c.rB_t[:, :]))
            wt(v, m)
            for ci in range(DC):
                m = c.DV.inc(v.tensor_sub(tf[:, ci * TB:(ci + 1) * TB],
                                          tf[:, ci * TB:(ci + 1) * TB], c.mB_t[:, :]))
            wt(v, m)
            st[f"ub{call}_{n}"] = c.DV.inc(v.tensor_copy(ub[:, :], tf[:, :]))
        blk.vector(_v_u)

    def emit_mm1(n):
        for m1 in range(FC):
            def _pe_h(t, n=n, m1=m1):
                if m1 == 0:
                    wt(t, st[f"ub{call}_{n}"], st["W1_all"])
                wt(t, st.get(f"hE{call}_{n}_{m1-2}") if m1 >= 2 else
                   st.get(f"hE{call}_{n-1}_{FC-2+m1}"))
                mm = None
                for k in range(DC):
                    mm = t.matmul(pH[m1 % 2][:, :],
                                  c.W1b[:, k * FF + 128 * m1:k * FF + 128 * (m1 + 1)],
                                  ub[:, k * TB:(k + 1) * TB],
                                  start=(k == 0), stop=(k == DC - 1))
                st[f"h{call}_{n}_{m1}"] = c.PE.inc(mm)
                if m1 == FC - 1:
                    st[f"h{call}_{n}_last"] = st[f"h{call}_{n}_{m1}"]
            blk.tensor(_pe_h)

            def _sc_h(s, n=n, m1=m1):
                if m1 == 0:
                    wt(s, st["b1sel"])
                    if n == 0 and not skip_stats:
                        wt(s, st[f"pmm{call}_{NB-1}"])
                wt(s, st[f"h{call}_{n}_{m1}"], st.get(f"r2l{call}_{n-1}"))
                st[f"hE{call}_{n}_{m1}"] = c.SC.inc(s.activation(
                    hb[:, m1 * TB:(m1 + 1) * TB], pH[m1 % 2][:, :],
                    AF.Gelu, bias=c.b1sel[:, m1:m1 + 1]))
            blk.scalar(_sc_h)

    def emit_mm2(n):
        for m2 in range(DC):
            def _pe_r(t, n=n, m2=m2):
                if m2 == 0:
                    wt(t, st["W2_all"], st.get(f"l2mm{call}_{n-1}"),
                       st.get(f"l2e{call}_{n-1}"))
                    for k2 in range(FC):
                        wt(t, st[f"hE{call}_{n}_{k2}"])
                wt(t, st.get(f"rE{call}_{n}_{m2-2}") if m2 >= 2 else
                   st.get(f"rE{call}_{n-1}_{DC-2+m2}"))
                mm = None
                for k2 in range(FC):
                    mm = t.matmul(pR[m2 % 2][:, :],
                                  c.W2b[:, k2 * D + 128 * m2:k2 * D + 128 * (m2 + 1)],
                                  hb[:, k2 * TB:(k2 + 1) * TB],
                                  start=(k2 == 0), stop=(k2 == FC - 1))
                st[f"r2mm{call}_{n}_{m2}"] = c.PE.inc(mm)
                if m2 == DC - 1:
                    st[f"r2l{call}_{n}"] = st[f"r2mm{call}_{n}_{m2}"]
            blk.tensor(_pe_r)

            def _sc_r(s, n=n, m2=m2):
                wt(s, st[f"r2mm{call}_{n}_{m2}"], st.get(f"o3l{call}_{n-1}"))
                st[f"rE{call}_{n}_{m2}"] = c.SC.inc(
                    s.copy(rb[:, m2 * TB:(m2 + 1) * TB], pR[m2 % 2][:, :]))
            blk.scalar(_sc_r)

    def emit_mm3(n):
        slot = n % 2
        tf = c.tfp[slot]
        for m3 in range(DC):
            def _pe_o(t, n=n, m3=m3):
                if m3 == 0:
                    wt(t, st["wo"])
                    for k3 in range(DC):
                        wt(t, st[f"rE{call}_{n}_{k3}"])
                wt(t, st.get(f"xn{call}_{n}_{m3-2}") if m3 >= 2 else
                   st.get(f"xn{call}_{n-1}_{DC-2+m3}"))
                mm = None
                for k3 in range(DC):
                    mm = t.matmul(pO[m3 % 2][:, :],
                                  c.wob[:, k3 * D + 128 * m3:k3 * D + 128 * (m3 + 1)],
                                  rb[:, k3 * TB:(k3 + 1) * TB],
                                  start=(k3 == 0), stop=(k3 == DC - 1))
                st[f"o{call}_{n}_{m3}"] = c.PE.inc(mm)
                if m3 == DC - 1:
                    st[f"o3l{call}_{n}"] = st[f"o{call}_{n}_{m3}"]
            blk.tensor(_pe_o)

            def _sc_o(s, n=n, m3=m3):
                if m3 == 0:
                    wt(s, st["bo15"])
                wt(s, st[f"o{call}_{n}_{m3}"])
                st[f"oE{call}_{n}_{m3}"] = c.SC.inc(s.activation(
                    pO[m3 % 2][:, :], pO[m3 % 2][:, :], AF.Identity,
                    bias=c.bo15[:, m3:m3 + 1], scale=0.15))
            blk.scalar(_sc_o)

            def _v_xn(v, n=n, m3=m3, slot=slot, tf=tf):
                wt(v, st[f"oE{call}_{n}_{m3}"], st[f"ub{call}_{n}"])
                m = c.DV.inc(v.scalar_tensor_tensor(
                    tf[:, m3 * TB:(m3 + 1) * TB], tf[:, m3 * TB:(m3 + 1) * TB], 0.5,
                    xb[slot][:, m3 * TB:(m3 + 1) * TB], ALU.mult, ALU.add))
                wt(v, m)
                st[f"xn{call}_{n}_{m3}"] = c.DV.inc(v.tensor_add(
                    tf[:, m3 * TB:(m3 + 1) * TB], tf[:, m3 * TB:(m3 + 1) * TB],
                    pO[m3 % 2][:, :]))
            blk.vector(_v_xn)

            def _gp_st(g, n=n, m3=m3, slot=slot, tf=tf):
                wt(g, st[f"xn{call}_{n}_{m3}"])
                st[f"tfst{call}_{n}_{m3}"] = c.GDS[slot * DC + m3].inc(
                    g.dma_start(dst_ap(n, m3), tf[:, m3 * TB:(m3 + 1) * TB]), 16)
            blk.gpsimd(_gp_st)
        st[f"stored{call}_{n}"] = st[f"tfst{call}_{n}_{DC-1}"]

        if do_ln2:
            for ci in range(DC):
                def _sc_l2(s, n=n, ci=ci):
                    wt(s, st[f"xn{call}_{n}_{ci}"],
                       st.get(f"l2q{call}_{n}_{ci-1}"),
                       lnst[0] if ci == 0 else None)
                    st[f"l2s{call}_{n}_{ci}"] = c.SC.inc(s.activation(
                        sqs[:, :], tf[:, ci * TB:(ci + 1) * TB], AF.Square))
                blk.scalar(_sc_l2)

                def _pe_l2(t, n=n, ci=ci):
                    if ci == 0:
                        wt(t, st.get(f"l2e{call}_{n-1}"), st[f"o3l{call}_{n}"])
                    wt(t, st[f"l2s{call}_{n}_{ci}"], st[f"xn{call}_{n}_{ci}"])
                    t.matmul(pS[0:1, :], c.ones128f[:, :], tf[:, ci * TB:(ci + 1) * TB],
                             start=(ci == 0), stop=(ci == DC - 1))
                    mm = t.matmul(pR[0][0:1, :], c.ones128b[:, :], sqs[:, :],
                                  start=(ci == 0), stop=(ci == DC - 1))
                    st[f"l2q{call}_{n}_{ci}"] = c.PE.inc(mm)
                blk.tensor(_pe_l2)
            st[f"l2mm{call}_{n}"] = st[f"l2q{call}_{n}_{DC-1}"]

            def _v_l2(v, n=n):
                wt(v, st[f"l2mm{call}_{n}"], st.get(f"l2st{call}_{n-1}"))
                c.DV.inc(v.tensor_copy(c.lnrow[0:1, :], pS[0:1, :]))
                st[f"l2e{call}_{n}"] = c.DV.inc(v.tensor_copy(c.lnrow[32:33, :], pR[0][0:1, :]))
            blk.vector(_v_l2)

            def _gp_l2(g, n=n):
                wt(g, st[f"l2e{call}_{n}"])
                c.GDL.inc(g.dma_start(bass.AP(c.ln2dh, n * TB, [[TSH, 1], [1, TB]]),
                                      c.lnrow[0:1, :]), 16)
                st[f"l2st{call}_{n}"] = c.GDL.inc(
                    g.dma_start(bass.AP(c.ln2dh, TSH + n * TB, [[TSH, 1], [1, TB]]),
                                c.lnrow[32:33, :]), 16)
            blk.gpsimd(_gp_l2)
            lnst[0] = st[f"l2st{call}_{n}"]
            st[f"tfwar{call}_{n}"] = [st[f"l2mm{call}_{n}"]] + \
                [st[f"tfst{call}_{n}_{m3}"] for m3 in range(DC)]
        else:
            st[f"tfwar{call}_{n}"] = [st[f"tfst{call}_{n}_{m3}"] for m3 in range(DC)]
        xb_last[n % 2] = st[f"xn{call}_{n}_{DC-1}"]

    emit_load(0)
    emit_bcast(0)
    emit_norm(0)
    for n in range(NB):
        if n + 1 < NB:
            emit_load(n + 1)
            emit_bcast(n + 1)
        emit_mm1(n)
        if n + 1 < NB:
            emit_norm(n + 1)
        emit_mm2(n)
        emit_mm3(n)
    st["ffn_done"] = st[f"stored{call}_{NB-1}"]
    st[f"gd_after_ffn{call}"] = [(s_, s_.v) for s_ in c.GDS] + [(c.GDL, c.GDL.v)]
    st["pX_free"] = bc_free
    st["xb_free"] = list(xb_last)


def _emit_finalize(c, blk, stats, dst, ms_rows, tag, rsum=0, rsq=32, dst_war=None):
    st = c.st

    def _g0(g):
        wt(g, ms_rows)
        c.GDL.inc(g.dma_start(c.fin0[:, :], stats[rsum:rsum + 1, :]), 16)
        st[f"{tag}_ld"] = c.GDL.inc(g.dma_start(c.fin1[:, :], stats[rsq:rsq + 1, :]), 16)
    blk.gpsimd(_g0)

    def _v(v):
        wt(v, st[f"{tag}_ld"])
        m = c.DV.inc(v.tensor_scalar_mul(c.fin0[:, :], c.fin0[:, :], 1.0 / D))
        m2 = c.DV.inc(v.tensor_scalar_mul(c.fin1[:, :], c.fin1[:, :], 1.0 / D))
        wt(v, m, m2)
        m = c.DV.inc(v.tensor_mul(c.fin2[:, :], c.fin0[:, :], c.fin0[:, :]))
        wt(v, m)
        m = c.DV.inc(v.tensor_sub(c.fin1[:, :], c.fin1[:, :], c.fin2[:, :]))
        wt(v, m)
        m = c.DV.inc(v.tensor_scalar_add(c.fin1[:, :], c.fin1[:, :], EPS))
        wt(v, m)
        st[f"{tag}_rv"] = c.DV.inc(v.reciprocal(c.fin1[:, :], c.fin1[:, :]))
    blk.vector(_v)

    def _s(s):
        wt(s, st[f"{tag}_rv"])
        st[f"{tag}_rs"] = c.SC.inc(s.activation(c.fin2[:, :], c.fin1[:, :], AF.Sqrt))
    blk.scalar(_s)

    def _v2(v):
        wt(v, st[f"{tag}_rs"])
        st[f"{tag}_ms"] = c.DV.inc(v.tensor_mul(c.fin3[:, :], c.fin0[:, :], c.fin2[:, :]))
    blk.vector(_v2)

    def _g1(g):
        wt(g, st[f"{tag}_ms"], dst_war)
        c.GDL.inc(g.dma_start(dst[0:1, :], c.fin3[:, :]), 16)
        st[f"{tag}_done"] = c.GDL.inc(g.dma_start(dst[1:2, :], c.fin2[:, :]), 16)
    blk.gpsimd(_g1)


def _emit_attn(c, blk, final):
    st = c.st
    xh = c.xb[0]
    rstdB = c.xb[1]
    u2f = c.tfp[0]
    u2b = c.ub
    a_t, nx_t, ml_t, P_t, y_t = c.a_t, c.nx_t, c.ml_t, c.P_t, c.y_t
    msB = c.stats1
    stats2 = c.stats2
    pH, pR, pO, pS, pX = c.pH, c.pR, c.pO, c.pS, c.pX

    def _sy0(s):
        wt(s, st["gd_after_ffn1"])
        c.LD.inc(s.dma_start(stats2[0:1, :], bass.AP(c.ln2dh, 0, [[TSH, 1], [1, TSH]])), 16)
        st["l2ld"] = c.LD.inc(s.dma_start(stats2[32:33, :],
                                          bass.AP(c.ln2dh, TSH, [[TSH, 1], [1, TSH]])), 16)
    blk.sync(_sy0)

    _emit_finalize(c, blk, stats2, stats2, st["l2ld"], "fa")
    st["rmv2"] = st["fa_done"]

    for i in range(2 * NB):
        isr = i < NB
        nb = i if isr else i - NB
        bank = pX if i % 2 == 0 else pS
        dstt = rstdB if isr else msB
        sel = c.selr[:, 128:256] if isr else c.selr[:, 0:128]

        def _pe_rb(t, i=i, nb=nb, bank=bank, sel=sel, war=st["pX_free"]):
            wt(t, st["rmv2"], war if i <= 1 else None, st.get(f"rbE{i-2}"))
            st[f"rbM{i}"] = c.PE.inc(t.matmul(
                bank[:, :], sel, stats2[0:2, nb * TB:(nb + 1) * TB],
                start=True, stop=True))
        blk.tensor(_pe_rb)

        def _sc_rb(s, i=i, nb=nb, bank=bank, dstt=dstt):
            wt(s, st[f"rbM{i}"])
            st[f"rbE{i}"] = c.SC.inc(s.copy(dstt[:, nb * TB:(nb + 1) * TB], bank[:, :]))
        blk.scalar(_sc_rb)
    st["rstdB"] = st[f"rbE{NB-1}"]
    st["msB"] = st[f"rbE{2*NB-1}"]
    st["pX_free"] = st["msB"]

    def S_load(h):
        def _sync(s, h=h):
            wt(s, st["gd_after_ffn1"], st.get(f"pa{h-1}"))
            st[f"ax{h}"] = c.LX[0].inc(s.dma_start(
                xh[:, :], bass.AP(c.x1sh, h * 128 * TSH, [[TSH, 128], [1, TSH]])), 16)
        blk.sync(_sync)

    def S_u2(h):
        def _v_u1(v, h=h):
            wt(v, st[f"ax{h}"], st["rstdB"], st.get(f"nx{h-1}"))
            st[f"u2m{h}"] = c.DV.inc(v.tensor_mul(u2f[:, :], xh[:, :], rstdB[:, :]))
        blk.vector(_v_u1)

        def _v_u2(v, h=h):
            wt(v, st[f"u2m{h}"], st["msB"])
            st[f"u2w{h}"] = c.DV.inc(v.tensor_sub(u2f[:, :], u2f[:, :], msB[:, :]))
        blk.vector(_v_u2)

        for q in range(4):
            def _v_ub(v, h=h, q=q):
                wt(v, st[f"u2w{h}"],
                   st.get(f"zx{h-1}_{NB-1}") if q == 0 else None)
                st[f"u2d{h}_{q}"] = c.SC.inc(v.copy(
                    u2b[:, 2 * q * TB:(2 * q + 2) * TB],
                    u2f[:, 2 * q * TB:(2 * q + 2) * TB]))
            blk.scalar(_v_ub)
        st[f"u2d{h}"] = st[f"u2d{h}_3"]

    def S_gates(h):
        for q in range(4):
            for i in range(2):
                nb = 2 * q + i

                def _pe_g(t, h=h, nb=nb, i=i, q=q):
                    wt(t, st[f"u2d{h}_{q}"])
                    if nb == 0:
                        wt(t, st["wab"])
                    if i == 0:
                        wt(t, st.get(f"taE{h}_{q-1}") if q >= 1 else st.get(f"txE{h-1}_3"))
                        wt(t, st.get(f"txE{h}_{q-1}") if q >= 1 else None)
                    st[f"za{h}_{nb}"] = c.PE.inc(t.matmul(
                        c.pBig[:, i * TB:(i + 1) * TB],
                        c.wab[:, h * 128:(h + 1) * 128],
                        u2b[:, nb * TB:(nb + 1) * TB], start=True, stop=True))
                    st[f"zx{h}_{nb}"] = c.PE.inc(t.matmul(
                        c.pBig[:, (2 + i) * TB:(3 + i) * TB],
                        c.winb[:, h * 128:(h + 1) * 128],
                        u2b[:, nb * TB:(nb + 1) * TB], start=True, stop=True))
                blk.tensor(_pe_g)

            def _sc_g(s, h=h, q=q):
                if q == 0:
                    wt(s, st["binh"], st["bahh"], st.get(f"sP{h-1}"), st.get(f"s1{h-1}"))
                wt(s, st[f"zx{h}_{2*q+1}"])
                st[f"taE{h}_{q}"] = c.SC.inc(s.activation(
                    a_t[:, 2 * q * TB:(2 * q + 2) * TB], c.pBig[:, 0:2 * TB],
                    AF.Tanh, scale=0.5, bias=c.bahh[:, h:h + 1]))
                st[f"txE{h}_{q}"] = c.SC.inc(s.activation(
                    nx_t[:, 2 * q * TB:(2 * q + 2) * TB], c.pBig[:, 2 * TB:4 * TB],
                    AF.Tanh, scale=0.5, bias=c.binh[:, h:h + 1]))
            blk.scalar(_sc_g)

    def S_pre(h):
        def _sc_a(s, h=h):
            wt(s, st[f"taE{h}_3"], st["sp4n"], st.get(f"pt{h-1}"),
               st.get(f"nx{h-1}"), st.get(f"cci{h-1}"), st["ones"])
            m = c.SC.inc(s.activation(a_t[:, :], a_t[:, :], AF.Identity,
                                      bias=c.sp4n[:, h:h + 1], scale=c.sp4n[:, h:h + 1],
                                      accum_out=c.sumlog[:, 0:1]))
            st[f"la{h}"] = m
            wt(s, m)
            m = c.SC.inc(s.activation(ml_t[:, :], a_t[:, :], AF.Exp, scale=2.0))
            st[f"e2{h}"] = m
            wt(s, m)
            st[f"mlt{h}"] = c.SC.inc(s.activation(ml_t[:, :], ml_t[:, :], AF.Sqrt,
                                                  scale=-1.0, bias=c.ones128f[:, :1]))
            wt(s, st[f"e2{h}"])
            st[f"a{h}"] = c.SC.inc(s.activation(a_t[:, :], a_t[:, :], AF.Exp))
            wt(s, st[f"la{h}"])
            st[f"pt{h}"] = c.SC.inc(s.activation(c.summ[:, 0:1], c.sumlog[:, 0:1], AF.Exp))
        blk.scalar(_sc_a)

        def _v_rs(v, h=h):
            wt(v, st[f"mlt{h}"], st["omr0"], st["metaB"])
            st[f"m0{h}"] = c.DV.inc(v.scalar_tensor_tensor(
                ml_t[:, 0:1], ml_t[:, 0:1], c.omr0[:, 0:1], c.metaB[:, 0:1],
                ALU.mult, ALU.add))
        blk.vector(_v_rs)

        def _v_nx(v, h=h):
            wt(v, st[f"m0{h}"])
            m = c.DV.inc(v.scalar_tensor_tensor(u2f[:, :], ml_t[:, :], 0.5, u2f[:, :],
                                                ALU.mult, ALU.mult))
            wt(v, m, st[f"txE{h}_3"])
            st[f"nx{h}"] = c.DV.inc(v.scalar_tensor_tensor(nx_t[:, :], nx_t[:, :], 1.0,
                                                           u2f[:, :], ALU.add, ALU.mult))
            wt(v, st[f"a{h}"])
            st[f"a0{h}"] = c.DV.inc(v.tensor_scalar(a_t[:, 0:1], a_t[:, 0:1],
                                                    c.omr0[:, 0:1], None, ALU.mult))
        blk.vector(_v_nx)

    def S_scan(h):
        def _v_s1(v, h=h):
            wt(v, st[f"nx{h}"], st[f"a0{h}"], st.get(f"x2st{h-1}"), st.get(f"l3s{h-1}"),
               st.get(f"x2c{h-1}"), st.get(f"l3m{h-1}"), st.get(f"l3x{h-1}"))
            st[f"s1{h}"] = c.DV.inc(v.tensor_tensor_scan(y_t[:, :], a_t[:, :], nx_t[:, :],
                                                         0.0, ALU.mult, ALU.add))
            wt(v, st[f"s1{h}"], st.get(f"cci{h-1}"))
            st[f"yl{h}"] = c.DV.inc(v.tensor_copy(c.summ[:, 1:2], y_t[:, TSH - 1:TSH]))
            wt(v, st[f"yl{h}"])
            st[f"pa{h}"] = c.DV.inc(v.tensor_add(y_t[:, :], y_t[:, :], xh[:, :]))
            wt(v, st[f"nx{h}"], st.get(f"l3P{h-1}"))
            st[f"sP{h}"] = c.DV.inc(v.tensor_tensor_scan(P_t[:, :], a_t[:, :], a_t[:, :],
                                                         1.0, ALU.mult, ALU.bypass))
        blk.vector(_v_s1)

    def S_cc(h):
        def _gp_cc(g, h=h):
            wt(g, st[f"yl{h}"], st[f"pt{h}"])
            st[f"cci{h}"] = c.GDA.inc(g.dma_start(c.ccin[h].ap()[:, :], c.summ[:, :]), 16)
            wt(g, st[f"cci{h}"])
            m = c.CC.inc(g.collective_compute(
                "AllGather", ALU.bypass, replica_groups=[list(range(NCORE))],
                ins=[c.ccin[h].ap().opt()], outs=[c.ccout[h].ap().opt()]))
            wt(g, m, st.get(f"carry{h-1}"))
            mm = None
            for j in range(NCORE):
                mm = c.GDA.inc(g.dma_start(c.ag_t[:, 2 * j:2 * j + 2],
                                           c.ccout[h].ap()[128 * j:128 * (j + 1), :]), 16)
            st[f"ag{h}"] = mm
        blk.gpsimd(_gp_cc)

    def S_fold(h):
        def _v_cf(v, h=h):
            wt(v, st[f"ag{h}"], st["omask"], st["metaB"])
            m = c.DV.inc(v.tensor_mul(c.peff[:, :], _strided(c.ag_t, 0), c.metaB[:, 1:9]))
            wt(v, m)
            m = c.DV.inc(v.tensor_add(c.peff[:, :], c.peff[:, :], c.omask[:, :]))
            my = c.DV.inc(v.tensor_mul(c.yeff[:, :], _strided(c.ag_t, 1), c.metaB[:, 1:9]))
            wt(v, m, my)
            m = c.DV.inc(v.tensor_copy(c.carry[:, :], c.yeff[:, 0:1]))
            for j in range(1, 7):
                wt(v, m)
                m = c.DV.inc(v.scalar_tensor_tensor(
                    c.carry[:, :], c.carry[:, :], c.peff[:, j:j + 1],
                    c.yeff[:, j:j + 1], ALU.mult, ALU.add))
            st[f"carry{h}"] = m
        blk.vector(_v_cf)

    def S_fix(h):
        def _v_fx(v, h=h):
            wt(v, st[f"carry{h}"], st[f"sP{h}"])
            st[f"fx2{h}"] = c.DV.inc(v.scalar_tensor_tensor(
                y_t[:, :], P_t[:, :], c.carry[:, 0:1], y_t[:, :], ALU.mult, ALU.add))
            wt(v, st[f"fx2{h}"])
            st[f"x2c{h}"] = c.DV.inc(v.tensor_copy(c.x2last[:, h:h + 1],
                                                   y_t[:, TSH - 1:TSH]))
        blk.vector(_v_fx)

        def _sc_l3(s, h=h):
            wt(s, st[f"fx2{h}"])
            st[f"l3s{h}"] = c.SC.inc(s.activation(P_t[:, :], y_t[:, :], AF.Square))
        blk.scalar(_sc_l3)

        def _pe_l3x(t, h=h):
            wt(t, st[f"fx2{h}"], st["msB"])
            if h > 0:
                wt(t, st.get(f"l3x{h-1}"))
            mm = None
            for nb in range(NB):
                bank = pS if nb < 4 else pX
                row = 32 * (nb % 4)
                mm = t.matmul(bank[row:row + 1, :], c.ones128f[:, :],
                              y_t[:, nb * TB:(nb + 1) * TB],
                              start=(h == 0), stop=(h == H - 1),
                              tile_position=(0, row))
            st[f"l3x{h}"] = c.PE.inc(mm)
        blk.tensor(_pe_l3x)

        def _pe_l3q(t, h=h):
            wt(t, st[f"l3s{h}"], st["msB"])
            if h > 0:
                wt(t, st.get(f"l3P{h-1}"))
            mm = None
            for nb in range(NB):
                sl = c.pBig2[:, 0:TB] if nb < 4 else c.pBig2[:, TB:2 * TB]
                row = 32 * (nb % 4)
                mm = t.matmul(sl[row:row + 1, :], c.ones128f[:, :],
                              P_t[:, nb * TB:(nb + 1) * TB],
                              start=(h == 0), stop=(h == H - 1),
                              tile_position=(0, row))
            st[f"l3P{h}"] = c.PE.inc(mm)
        blk.tensor(_pe_l3q)
        st[f"l3m{h}"] = st[f"l3P{h}"]

        def _gp_x2(g, h=h):
            wt(g, st[f"fx2{h}"])
            st[f"x2st{h}"] = c.GDX.inc(g.dma_start(
                bass.AP(c.x2sh if not final else c.outh, h * 128 * TSH,
                        [[TSH, 128], [1, TSH]]), y_t[:, :]), 16)
        blk.gpsimd(_gp_x2)

    S_load(0)
    S_u2(0)
    S_gates(0)
    S_pre(0)
    S_scan(0)
    S_cc(0)
    for h in range(H):
        if h + 1 < H:
            S_load(h + 1)
            S_u2(h + 1)
            S_gates(h + 1)
            S_pre(h + 1)
        S_fold(h)
        S_fix(h)
        if h + 1 < H:
            S_scan(h + 1)
            S_cc(h + 1)

    def _v_l3e(v):
        wt(v, st[f"l3x{H-1}"], st[f"l3m{H-1}"], st["fa_ld"])
        m = None
        for nb in range(NB):
            bank = pS if nb < 4 else pX
            row = 32 * (nb % 4)
            m = c.DV.inc(v.tensor_copy(stats2[32:33, nb * TB:(nb + 1) * TB],
                                       bank[row:row + 1, :]))
            sl = c.pBig2[:, 0:TB] if nb < 4 else c.pBig2[:, TB:2 * TB]
            m = c.DV.inc(v.tensor_copy(stats2[64:65, nb * TB:(nb + 1) * TB],
                                       sl[row:row + 1, :]))
        st["l3xe"] = m
    blk.vector(_v_l3e)

    _emit_finalize(c, blk, stats2, c.stats1, st["l3xe"], "f3", rsum=32, rsq=64,
                   dst_war=st[f"u2w{H-1}"])
    st["rmv_ffn2"] = st["f3_done"]
    st["x2last_ms"] = st[f"x2c{H-1}"]
    st["gd_after_attn"] = [(c.GDX, c.GDX.v), (c.GDL, c.GDL.v)]
    st["xb_free"] = [st[f"x2st{H-1}"], st["msB"]]


def _strided(t, off):
    return bass.AP(t, off, [[16, 128], [2, 8]])



_NC_CACHE = {}


def _get_nc():
    if "nc" not in _NC_CACHE:
        _NC_CACHE["nc"] = build_program()
    return _NC_CACHE["nc"]


def kernel(L0_attn_x, L0_attn_w_in, L0_attn_b_in, L0_attn_w_a, L0_attn_b_a,
           L0_attn_a_param, L0_ffn_Wp1, L0_ffn_bp1, L0_ffn_Wp2, L0_ffn_bp2,
           L0_ffn_W1, L0_ffn_b1, L0_ffn_W2, L0_ffn_b2, L0_ffn_Wo, L0_ffn_bo):
    x = np.asarray(L0_attn_x, dtype=np.float32)
    xfm = np.ascontiguousarray(x.transpose(0, 2, 1))
    W1 = np.ascontiguousarray(np.asarray(L0_ffn_W1, np.float32))
    W2 = np.ascontiguousarray(np.asarray(L0_ffn_W2, np.float32))
    Wo = np.ascontiguousarray(np.asarray(L0_ffn_Wo, np.float32))
    wp1 = np.ascontiguousarray(np.asarray(L0_ffn_Wp1, np.float32))
    wp2 = np.ascontiguousarray(np.asarray(L0_ffn_Wp2, np.float32))
    bp1 = np.ascontiguousarray(np.asarray(L0_ffn_bp1, np.float32).reshape(128, 1))
    bp2 = np.ascontiguousarray(np.asarray(L0_ffn_bp2, np.float32).reshape(E, 1))
    b1cm = np.ascontiguousarray(np.asarray(L0_ffn_b1, np.float32).reshape(E, FC, 128)
                                .transpose(2, 0, 1).reshape(128, E * FC))
    b2cm = np.ascontiguousarray(np.asarray(L0_ffn_b2, np.float32).reshape(E, DC, 128)
                                .transpose(2, 0, 1).reshape(128, E * DC))
    bocm = np.ascontiguousarray(np.asarray(L0_ffn_bo, np.float32).reshape(DC, 128).T)
    w_in = np.ascontiguousarray(np.asarray(L0_attn_w_in, np.float32))
    w_a = np.ascontiguousarray(np.asarray(L0_attn_w_a, np.float32))
    bincm = np.ascontiguousarray(np.asarray(L0_attn_b_in, np.float32).T)
    bacm = np.ascontiguousarray(np.asarray(L0_attn_b_a, np.float32).T)
    aparcm = np.ascontiguousarray(np.asarray(L0_attn_a_param, np.float32).reshape(H, 128).T)

    in_maps = []
    for core in range(NCORE):
        b, q = core // QPB, core % QPB
        shard = np.ascontiguousarray(xfm[b, :, q * TSH:(q + 1) * TSH])
        lastrow = np.ascontiguousarray(x[b, -1, :].reshape(DC, 128).T)
        meta = np.zeros((1, 32), dtype=np.float32)
        meta[0, 0] = 1.0 if q == 0 else 0.0
        for j in range(8):
            meta[0, 1 + j] = 1.0 if (j // QPB == b and j % QPB < q) else 0.0
        for j in range(8):
            meta[0, 9 + j] = 1.0 if j == (b * QPB + QPB - 1) else 0.0
        selrows = np.zeros((2, 256), dtype=np.float32)
        selrows[0, 0:128] = 1.0
        selrows[1, 128:256] = 1.0
        in_maps.append({
            "x0": shard, "lastrow": lastrow, "meta": meta, "selrows": selrows,
            "W1": W1, "W2": W2, "Wo": Wo,
            "Wp1": wp1, "Wp2": wp2, "bp1": bp1, "bp2": bp2,
            "b1cm": b1cm, "b2cm": b2cm, "bocm": bocm,
            "w_in": w_in, "w_a": w_a, "bincm": bincm, "bacm": bacm,
            "aparcm": aparcm,
        })

    nc = _get_nc()
    res = run_bass_kernel_spmd(nc, in_maps, core_ids=list(range(NCORE)))
    out = np.empty((B, T, D), dtype=np.float32)
    for core in range(NCORE):
        b, q = core // QPB, core % QPB
        out[b, q * TSH:(q + 1) * TSH, :] = np.asarray(res.results[core]["y"]).T
    return out


# version 20
# speedup vs baseline: 1.0028x; 1.0028x over previous
import contextlib
import numpy as np
import concourse.bass as bass
import concourse.mybir as mybir
from concourse.bass_utils import run_bass_kernel_spmd

F32 = mybir.dt.float32
BF16 = mybir.dt.bfloat16
I32 = mybir.dt.int32
AF = mybir.ActivationFunctionType
ALU = mybir.AluOpType
AX = mybir.AxisListType

B = 2
T = 16384
D = 1024
NCORE = 8
QPB = 4
TSH = T // QPB
NB = 8
TB = TSH // NB
DC = 8
FC = 16
H = 8
E = 8
FF = 2048
EPS = 1e-5

STAGE = 3
DEBUG = False


class SemC:
    def __init__(self, es, nc, name):
        self.h = es.enter_context(nc.semaphore(name))
        self.v = 0

    def inc(self, inst, n=1):
        inst.then_inc(self.h, n)
        self.v += n
        return (self, self.v)


def wt(eng, *ms):
    best = {}
    for m in ms:
        if m is None:
            continue
        if isinstance(m, (list, tuple)) and m and isinstance(m[0], (list, tuple)):
            for mm in m:
                if mm is None:
                    continue
                sem, val = mm
                if val > best.get(id(sem), (None, 0))[1]:
                    best[id(sem)] = (sem, val)
            continue
        sem, val = m
        if val > best.get(id(sem), (None, 0))[1]:
            best[id(sem)] = (sem, val)
    for sem, val in best.values():
        eng.wait_ge(sem.h, val)


class Ctx:
    pass


class _MockInst:
    __slots__ = ("calls",)

    def __init__(self):
        self.calls = []

    def then_inc(self, sem, n=1):
        self.calls.append((sem, n))
        return self


class _EngProxy:
    def __init__(self, rec):
        self._rec = rec

    def __getattr__(self, attr):
        rec = self._rec

        def call(*a, **k):
            mi = _MockInst()
            rec.append((attr, a, k, mi))
            return mi
        return call


class DeferredBlock:
    ENGS = ("sync", "scalar", "vector", "tensor", "gpsimd")

    def __init__(self):
        self.rec = {e: [] for e in self.ENGS}

    def _mk(self, name):
        def run(fn):
            fn(_EngProxy(self.rec[name]))
        return run

    def __getattr__(self, name):
        if name in self.ENGS:
            return self._mk(name)
        raise AttributeError(name)

    def finalize(self, blk):
        for e in self.ENGS:
            ops = self.rec[e]
            if not ops:
                continue

            def _replay(eng, ops=ops):
                for attr, a, k, mi in ops:
                    inst = getattr(eng, attr)(*a, **k)
                    for sem, n in mi.calls:
                        inst.then_inc(sem, n)
            getattr(blk, e)(_replay)


def build_program():
    nc = bass.Bass("TRN2", target_bir_lowering=False)
    c = Ctx()
    c.nc = nc
    c.st = {}

    c.x0h = nc.dram_tensor("x0", [D, TSH], F32, kind="ExternalInput")
    c.outh = nc.dram_tensor("y", [D, TSH], F32, kind="ExternalOutput")
    c.lrh = nc.dram_tensor("lastrow", [128, DC], F32, kind="ExternalInput")
    c.metah = nc.dram_tensor("meta", [1, 32], F32, kind="ExternalInput")
    c.W1h = nc.dram_tensor("W1", [E, D, FF], F32, kind="ExternalInput")
    c.W2h = nc.dram_tensor("W2", [E, FF, D], F32, kind="ExternalInput")
    c.Woh = nc.dram_tensor("Wo", [D, D], F32, kind="ExternalInput")
    c.wp1h = nc.dram_tensor("Wp1", [D, 128], F32, kind="ExternalInput")
    c.wp2h = nc.dram_tensor("Wp2", [128, E], F32, kind="ExternalInput")
    c.bp1h = nc.dram_tensor("bp1", [128, 1], F32, kind="ExternalInput")
    c.bp2h = nc.dram_tensor("bp2", [E, 1], F32, kind="ExternalInput")
    c.b1cmh = nc.dram_tensor("b1cm", [128, E * FC], F32, kind="ExternalInput")
    c.b2cmh = nc.dram_tensor("b2cm", [128, E * DC], F32, kind="ExternalInput")
    c.bocmh = nc.dram_tensor("bocm", [128, DC], F32, kind="ExternalInput")
    c.winh = nc.dram_tensor("w_in", [H, 128, 128], F32, kind="ExternalInput")
    c.wah = nc.dram_tensor("w_a", [H, 128, 128], F32, kind="ExternalInput")
    c.bincmh = nc.dram_tensor("bincm", [128, H], F32, kind="ExternalInput")
    c.bacmh = nc.dram_tensor("bacm", [128, H], F32, kind="ExternalInput")
    c.aparh = nc.dram_tensor("aparcm", [128, H], F32, kind="ExternalInput")
    c.selrh = nc.dram_tensor("selrows", [2, 256], F32, kind="ExternalInput")
    c.x1sh = nc.dram_tensor("x1s", [D, TSH], F32)
    c.x2sh = nc.dram_tensor("x2s", [D, TSH], F32)
    c.ln2dh = nc.dram_tensor("ln2d", [2, TSH], F32)
    c.ccin = [nc.dram_tensor(f"ccin{h}", [128, 2], F32) for h in range(H)]
    c.ccout = [nc.dram_tensor(f"ccout{h}", [NCORE * 128, 2], F32, addr_space="Shared")
               for h in range(H)]
    c.cc2in = nc.dram_tensor("cc2in", [1, 8], F32)
    c.cc2out = nc.dram_tensor("cc2out", [NCORE, 8], F32, addr_space="Shared")
    if DEBUG:
        c.dbgh = nc.dram_tensor("dbg", [128, 64], F32, kind="ExternalOutput")
        c.dbg2h = nc.dram_tensor("dbg2", [2, TSH], F32, kind="ExternalOutput")

    es = contextlib.ExitStack()
    with es:
        c.LD = SemC(es, nc, "LD")
        c.LX = [SemC(es, nc, "LX0"), SemC(es, nc, "LX1")]
        c.GD = SemC(es, nc, "GD")
        c.GD2 = SemC(es, nc, "GD2")
        c.GDC = SemC(es, nc, "GDC")
        c.GDS = [SemC(es, nc, f"GDS{i}") for i in range(2 * DC)]
        c.GDX = SemC(es, nc, "GDX")
        c.GDL = SemC(es, nc, "GDL")
        c.GDA = SemC(es, nc, "GDA")
        c.CC = SemC(es, nc, "CC")
        c.PE = SemC(es, nc, "PE")
        c.DV = SemC(es, nc, "DV")
        c.SC = SemC(es, nc, "SC")
        c.PO = SemC(es, nc, "PO")

        def sb(name, shape, dt=F32):
            return es.enter_context(nc.sbuf_tensor(name, shape, dt))

        c.wob = sb("wob", [128, DC * D], BF16)
        c.winb = sb("winb", [128, H * 128], BF16)
        c.wab = sb("wab", [128, H * 128], BF16)
        c.wp2f = sb("wp2f", [128, E], F32)
        c.bp1c = sb("bp1c", [128, 1], F32)
        c.bp2c = sb("bp2c", [E, 1], F32)
        c.selmk = sb("selmk", [E, 1], F32)
        c.b1cm = sb("b1cm_s", [128, E * FC], F32)
        c.b2cm = sb("b2cm_s", [128, E * DC], F32)
        c.bocm = sb("bocm_s", [128, DC], F32)
        c.bincm = sb("bincm_s", [128, H], F32)
        c.bacm = sb("bacm_s", [128, H], F32)
        c.aparcm = sb("aparcm_s", [128, H], F32)
        c.sp4n = sb("sp4n", [128, H], F32)
        c.binh = sb("binh", [128, H], F32)
        c.bahh = sb("bahh", [128, H], F32)
        c.selr = sb("selr", [2, 256], F32)
        c.lr_t = sb("lr_t", [128, DC], F32)
        c.meta_t = sb("meta_t", [1, 32], F32)
        c.metaB = sb("metaB", [128, 32], F32)
        c.omask = sb("omask", [128, 8], F32)
        c.omr0 = sb("omr0", [128, 1], F32)
        c.ones128f = sb("ones128f", [128, 1], F32)
        c.ones128b = sb("ones128b", [128, 1], BF16)
        c.ones1x128 = sb("ones1x128", [1, 128], F32)
        c.io7mi = sb("io7mi", [1, 8], F32)
        c.ioW1 = sb("ioW1", [128, DC], F32)
        c.ioW2 = sb("ioW2", [128, FC], F32)
        c.rt_sq = sb("rt_sq", [128, DC], F32)
        c.rt_pk = sb("rt_pk", [128, 2], F32)
        c.rt_row = sb("rt_row", [1, 2], F32)
        c.rt_m = sb("rt_m", [1, 1], F32)
        c.rt_q = sb("rt_q", [1, 1], F32)
        c.rt_s = sb("rt_s", [1, 1], F32)
        c.rt_u = sb("rt_u", [128, DC], F32)
        c.rt_zg = sb("rt_zg", [128, 1], F32)
        c.rt_lg = sb("rt_lg", [E, 1], F32)
        c.rt_lgrow = sb("rt_lgrow", [1, E], F32)
        c.rt_mx = sb("rt_mx", [1, 1], F32)
        c.rt_oh = sb("rt_oh", [1, E], F32)
        c.rt_ag = sb("rt_ag", [E, E], F32)
        c.rt_selB = sb("rt_selB", [128, 1], F32)
        c.rt_ohB = sb("rt_ohB", [128, E], F32)
        c.idxW1f = sb("idxW1f", [128, DC], F32)
        c.idxW2f = sb("idxW2f", [128, FC], F32)
        c.idxW1 = sb("idxW1", [128, DC], I32)
        c.idxW2 = sb("idxW2", [128, FC], I32)
        c.b1sel = sb("b1sel", [128, FC], F32)
        c.b2sel = sb("b2sel", [128, DC], F32)
        c.b2selb = sb("b2selb", [128, DC], BF16)
        c.bo15 = sb("bo15", [128, DC], F32)
        c.stats1 = sb("stats1", [128, TSH], F32)
        c.rB_t = sb("rB_t", [128, TB], F32)
        c.mB_t = sb("mB_t", [128, TB], F32)
        c.lnrow = sb("lnrow", [128, TB], F32)
        c.summ = sb("summ", [128, 2], F32)
        c.ag_t = sb("ag_t", [128, 16], F32)
        c.peff = sb("peff", [128, 8], F32)
        c.yeff = sb("yeff", [128, 8], F32)
        c.carry = sb("carry", [128, 1], F32)
        c.x2last = sb("x2last", [128, DC], F32)
        c.sumlog = sb("sumlog", [128, 1], F32)
        if DEBUG:
            c.dbg_t = sb("dbg_t", [128, 64], F32)
        c.fin0 = sb("fin0", [128, 32], F32)
        c.fin1 = sb("fin1", [128, 32], F32)
        c.fin2 = sb("fin2", [128, 32], F32)
        c.fin3 = sb("fin3", [128, 32], F32)
        c.ccol = sb("ccol", [128, 1], F32)
        c.ccol2 = sb("ccol2", [128, 1], F32)
        c.xb = [sb("xb0", [128, TSH], F32), sb("xb1", [128, TSH], F32)]
        c.tfp = [sb("tf0", [128, TSH], F32), sb("tf1", [128, TSH], F32)]
        c.ub = sb("ub", [128, TSH], BF16)

        c.pBig = es.enter_context(nc.psum_tensor("pBig", [128, 4 * TB], F32))
        c.pBig2 = es.enter_context(nc.psum_tensor("pBig2", [128, 2 * TB], F32))
        c.pH = [c.pBig[:, 0:TB], c.pBig[:, TB:2 * TB]]
        c.pR = [c.pBig[:, 2 * TB:3 * TB], c.pBig[:, 3 * TB:4 * TB]]
        c.pO = [c.pBig2[:, 0:TB], c.pBig2[:, TB:2 * TB]]
        c.pS = es.enter_context(nc.psum_tensor("pS", [128, TB], F32))
        c.pX = es.enter_context(nc.psum_tensor("pX", [128, TB], F32))

        with contextlib.ExitStack() as es1:
            with contextlib.ExitStack() as esr:
                c.wp1f = esr.enter_context(nc.sbuf_tensor("wp1f_a", [128, DC * 128], F32))
            c.W1b = es1.enter_context(nc.sbuf_tensor("W1b_a", [128, DC * FF], BF16))
            c.W2b = es1.enter_context(nc.sbuf_tensor("W2b_a", [128, FC * D], BF16))
            c.hb = es1.enter_context(nc.sbuf_tensor("hb_a", [128, FC * TB], BF16))
            c.rb = es1.enter_context(nc.sbuf_tensor("rb_a", [128, DC * TB], BF16))
            c.sqs = es1.enter_context(nc.sbuf_tensor("sqs_a", [128, TB], BF16))
            dfb = DeferredBlock()
            _emit_startup(c, dfb)
            _emit_router(c, dfb, which=1)
            _emit_gathers(c, dfb, which=1)
            _emit_ffn(c, dfb, call=1, srch=c.x0h,
                      dsth=(c.x1sh if STAGE >= 2 else c.outh), do_ln2=(STAGE >= 2))
            with nc.Block() as blk:
                dfb.finalize(blk)

        if STAGE >= 2:
            with contextlib.ExitStack() as es2:
                c.stats2 = es2.enter_context(nc.sbuf_tensor("stats2", [128, TSH], F32))
                c.a_t = es2.enter_context(nc.sbuf_tensor("a_t", [128, TSH], F32))
                c.nx_t = es2.enter_context(nc.sbuf_tensor("nx_t", [128, TSH], F32))
                c.ml_t = es2.enter_context(nc.sbuf_tensor("ml_t", [128, TSH], F32))
                c.P_t = es2.enter_context(nc.sbuf_tensor("P_t", [128, TSH], F32))
                c.y_t = c.tfp[1]
                dfb = DeferredBlock()
                _emit_attn(c, dfb, final=(STAGE == 2))
                with nc.Block() as blk:
                    dfb.finalize(blk)

        if STAGE >= 3:
            with contextlib.ExitStack() as es3:
                with contextlib.ExitStack() as esr3:
                    c.wp1f = esr3.enter_context(nc.sbuf_tensor("wp1f_b", [128, DC * 128], F32))
                c.W1b = es3.enter_context(nc.sbuf_tensor("W1b_b", [128, DC * FF], BF16))
                c.W2b = es3.enter_context(nc.sbuf_tensor("W2b_b", [128, FC * D], BF16))
                c.hb = es3.enter_context(nc.sbuf_tensor("hb_b", [128, FC * TB], BF16))
                c.rb = es3.enter_context(nc.sbuf_tensor("rb_b", [128, DC * TB], BF16))
                c.sqs = es3.enter_context(nc.sbuf_tensor("sqs_b", [128, TB], BF16))
                dfb = DeferredBlock()
                _load_wp1(c, dfb)
                _emit_router(c, dfb, which=2)
                _emit_gathers(c, dfb, which=2)
                _emit_ffn(c, dfb, call=2, srch=c.x2sh, dsth=c.outh, do_ln2=False,
                          skip_stats=True)
                with nc.Block() as blk:
                    dfb.finalize(blk)

        finals = [(s_, s_.v) for s_ in [c.GD, c.GD2, c.GDC, c.GDL, c.GDA, c.GDX, c.LD,
                                        c.LX[0], c.LX[1]] + c.GDS]
        with nc.Block() as blk:
            def _fin(g):
                wt(g, *finals)
            blk.gpsimd(_fin)

    return nc


def _load_wp1(c, blk):
    def _s(s):
        c.st["wp1"] = c.LD.inc(s.dma_start(
            c.wp1f[:, :], bass.AP(c.wp1h, 0, [[128, 128], [128 * 128, DC], [1, 128]])), 16)
    blk.sync(_s)


def _emit_startup(c, blk):
    st = c.st
    _load_wp1(c, blk)

    def _sync(s):
        st["lr"] = c.LD.inc(s.dma_start(c.lr_t[:, :], c.lrh.ap()[:, :]), 16)
        st["meta"] = c.LD.inc(s.dma_start(c.meta_t[:, :], c.metah.ap()[:, :]), 16)
        c.LD.inc(s.dma_start(c.wp2f[:, :], c.wp2h.ap()[:, :]), 16)
        c.LD.inc(s.dma_start(c.bp1c[:, :], c.bp1h.ap()[:, :]), 16)
        c.LD.inc(s.dma_start(c.bp2c[:, :], c.bp2h.ap()[:, :]), 16)
        c.LD.inc(s.dma_start(c.selmk[:, :], bass.AP(c.metah, 9, [[1, E], [1, 1]])), 16)
        c.LD.inc(s.dma_start(c.b1cm[:, :], c.b1cmh.ap()[:, :]), 16)
        c.LD.inc(s.dma_start(c.b2cm[:, :], c.b2cmh.ap()[:, :]), 16)
        c.LD.inc(s.dma_start(c.bocm[:, :], c.bocmh.ap()[:, :]), 16)
        c.LD.inc(s.dma_start(c.bincm[:, :], c.bincmh.ap()[:, :]), 16)
        c.LD.inc(s.dma_start(c.bacm[:, :], c.bacmh.ap()[:, :]), 16)
        c.LD.inc(s.dma_start(c.selr[:, :], c.selrh.ap()[:, :]), 16)
        st["consts"] = c.LD.inc(s.dma_start(c.aparcm[:, :], c.aparh.ap()[:, :]), 16)
        st["wp1"] = st["consts"]
    blk.sync(_sync)

    def _gp(g):
        c.GDC.inc(g.dma_start(c.winb[:, :],
                              bass.AP(c.winh, 0, [[128, 128], [128 * 128, H], [1, 128]])), 16)
        st["wab"] = c.GDC.inc(g.dma_start(
            c.wab[:, :], bass.AP(c.wah, 0, [[128, 128], [128 * 128, H], [1, 128]])), 16)
        m = None
        for k in range(DC):
            m = c.GDC.inc(g.dma_start(c.wob[:, k * D:(k + 1) * D],
                                      bass.AP(c.Woh, k * 128 * D, [[D, 128], [1, D]])), 16)
        st["wo"] = m
        st["wab"] = m
        st["io1"] = c.PO.inc(g.iota(c.ioW1[:, :], [[128, DC]], channel_multiplier=1,
                                    allow_small_or_imprecise_dtypes=True))
        st["io2"] = c.PO.inc(g.iota(c.ioW2[:, :], [[128, FC]], channel_multiplier=1,
                                    allow_small_or_imprecise_dtypes=True))
        st["io7"] = c.PO.inc(g.iota(c.io7mi[:, :], [[-1, 8]], base=7, channel_multiplier=0,
                                    allow_small_or_imprecise_dtypes=True))
    blk.gpsimd(_gp)

    def _v(v):
        c.DV.inc(v.memset(c.ones128f[:, :], 1.0))
        c.DV.inc(v.memset(c.ones128b[:, :], 1.0))
        st["ones"] = c.DV.inc(v.memset(c.ones1x128[:, :], 1.0))
    blk.vector(_v)

    def _sc(s):
        wt(s, st["consts"], st["ones"])
        m1 = c.SC.inc(s.activation(c.sp4n[:, :], c.aparcm[:, :], AF.Exp))
        wt(s, m1)
        st["sp"] = c.SC.inc(s.activation(c.sp4n[:, :], c.sp4n[:, :], AF.Ln,
                                         bias=c.ones128f[:, :1]))
    blk.scalar(_sc)

    def _v2(v):
        wt(v, st["sp"])
        st["sp4n"] = c.DV.inc(v.tensor_scalar_mul(c.sp4n[:, :], c.sp4n[:, :], -4.0))
        wt(v, st["consts"])
        st["binh"] = c.DV.inc(v.tensor_scalar_mul(c.binh[:, :], c.bincm[:, :], 0.5))
        st["bahh"] = c.DV.inc(v.tensor_scalar_mul(c.bahh[:, :], c.bacm[:, :], 0.5))
    blk.vector(_v2)

    def _pe(t):
        wt(t, st["consts"], st["ones"])
        st["metaB_mm"] = c.PE.inc(t.matmul(c.pX[:, 0:32], c.ones1x128[:, :], c.meta_t[:, :],
                                           start=True, stop=True))
    blk.tensor(_pe)

    def _v3(v):
        wt(v, st["metaB_mm"])
        m = c.DV.inc(v.tensor_copy(c.metaB[:, :], c.pX[:, 0:32]))
        st["metaB"] = m
        wt(v, m)
        st["omask"] = c.DV.inc(v.tensor_scalar(c.omask[:, :], c.metaB[:, 1:9], -1.0, 1.0,
                                               ALU.mult, ALU.add))
        st["omr0"] = c.DV.inc(v.tensor_scalar(c.omr0[:, :], c.metaB[:, 0:1], -1.0, 1.0,
                                              ALU.mult, ALU.add))
    blk.vector(_v3)
    st["pX_free"] = st["metaB"]


def _emit_router(c, blk, which):
    st = c.st
    src = c.lr_t if which == 1 else c.x2last
    ms_src = st["consts"] if which == 1 else st["x2last_ms"]

    def _sc1(s):
        wt(s, ms_src)
        st["r1"] = c.SC.inc(s.activation(c.rt_sq[:, :], src[:, :], AF.Square))
    blk.scalar(_sc1)

    def _v1(v):
        wt(v, st["r1"])
        c.DV.inc(v.tensor_reduce(c.rt_pk[:, 1:2], c.rt_sq[:, :], AX.X, ALU.add))
        wt(v, ms_src)
        st["r2"] = c.DV.inc(v.tensor_reduce(c.rt_pk[:, 0:1], src[:, :], AX.X, ALU.add))
    blk.vector(_v1)

    def _pe1(t):
        wt(t, st["r2"], st["pX_free"])
        st["r3"] = c.PE.inc(t.matmul(c.pX[0:1, 0:2], c.ones128f[:, :], c.rt_pk[:, :],
                                     start=True, stop=True))
    blk.tensor(_pe1)

    def _v2(v):
        wt(v, st["r3"])
        m = c.DV.inc(v.tensor_copy(c.rt_row[:, :], c.pX[0:1, 0:2]))
        wt(v, m)
        m = c.DV.inc(v.tensor_scalar_mul(c.rt_m[:, :], c.rt_row[:, 0:1], 1.0 / D))
        m2 = c.DV.inc(v.tensor_scalar_mul(c.rt_q[:, :], c.rt_row[:, 1:2], 1.0 / D))
        wt(v, m, m2)
        m = c.DV.inc(v.tensor_mul(c.rt_s[:, :], c.rt_m[:, :], c.rt_m[:, :]))
        wt(v, m)
        m = c.DV.inc(v.tensor_sub(c.rt_q[:, :], c.rt_q[:, :], c.rt_s[:, :]))
        wt(v, m)
        m = c.DV.inc(v.tensor_scalar_add(c.rt_q[:, :], c.rt_q[:, :], EPS))
        wt(v, m)
        st["r4"] = c.DV.inc(v.reciprocal(c.rt_q[:, :], c.rt_q[:, :]))
    blk.vector(_v2)

    def _sc2(s):
        wt(s, st["r4"])
        st["r5"] = c.SC.inc(s.activation(c.rt_s[:, :], c.rt_q[:, :], AF.Sqrt))
    blk.scalar(_sc2)

    def _pe2(t):
        wt(t, st["r5"])
        c.PE.inc(t.matmul(c.pX[:, 2:3], c.ones1x128[:, :], c.rt_s[:, :], start=True, stop=True))
        st["r6"] = c.PE.inc(t.matmul(c.pX[:, 3:4], c.ones1x128[:, :], c.rt_m[:, :],
                                     start=True, stop=True))
    blk.tensor(_pe2)

    def _v3(v):
        wt(v, st["r6"])
        m = c.DV.inc(v.tensor_copy(c.ccol[:, :], c.pX[:, 2:3]))
        m2 = c.DV.inc(v.tensor_copy(c.ccol2[:, :], c.pX[:, 3:4]))
        wt(v, m, m2, ms_src)
        m = c.DV.inc(v.tensor_scalar(c.rt_u[:, :], src[:, :], c.ccol2[:, 0:1], None,
                                     ALU.subtract))
        wt(v, m)
        st["r7"] = c.DV.inc(v.tensor_scalar(c.rt_u[:, :], c.rt_u[:, :], c.ccol[:, 0:1], None,
                                            ALU.mult))
    blk.vector(_v3)

    def _pe3(t):
        wt(t, st["r7"], st["wp1"])
        m = None
        for k in range(DC):
            m = c.PE.inc(t.matmul(c.pX[:, 4:5], c.wp1f[:, 128 * k:128 * (k + 1)],
                                  c.rt_u[:, k:k + 1], start=(k == 0), stop=(k == DC - 1)))
        st["r8"] = m
    blk.tensor(_pe3)

    def _sc3(s):
        wt(s, st["r8"])
        st["r9"] = c.SC.inc(s.activation(c.rt_zg[:, :], c.pX[:, 4:5], AF.Gelu,
                                         bias=c.bp1c[:, :1]))
    blk.scalar(_sc3)

    def _pe4(t):
        wt(t, st["r9"])
        st["r10"] = c.PE.inc(t.matmul(c.pX[0:E, 5:6], c.wp2f[:, :], c.rt_zg[:, :],
                                      start=True, stop=True))
    blk.tensor(_pe4)

    def _v4(v):
        wt(v, st["r10"], st["consts"])
        st["r11"] = c.DV.inc(v.tensor_add(c.rt_lg[:, :], c.pX[0:E, 5:6], c.bp2c[:, :]))
    blk.vector(_v4)

    def _gp1(g):
        wt(g, st["r11"])
        st["r12"] = c.GDA.inc(g.dma_start(c.rt_lgrow[:, :], c.rt_lg[:, :]), 16)
    blk.gpsimd(_gp1)

    def _v5(v):
        wt(v, st["r12"])
        m = c.DV.inc(v.reduce_max(c.rt_mx[:, :], c.rt_lgrow[:, :], axis=AX.X))
        wt(v, m)
        st["r13"] = c.DV.inc(v.tensor_scalar(c.rt_oh[:, :], c.rt_lgrow[:, :],
                                             c.rt_mx[:, 0:1], None, ALU.is_ge))
    blk.vector(_v5)

    if which == 2:
        def _gp2(g):
            wt(g, st["r13"])
            m = c.GDA.inc(g.dma_start(c.cc2in.ap()[:, :], c.rt_oh[:, :]), 16)
            wt(g, m)
            m = c.CC.inc(g.collective_compute(
                "AllGather", ALU.bypass, replica_groups=[list(range(NCORE))],
                ins=[c.cc2in.ap().opt()], outs=[c.cc2out.ap().opt()]))
            wt(g, m)
            st["r14"] = c.GDA.inc(g.dma_start(c.rt_ag[:, :], c.cc2out.ap()[:, :]), 16)
        blk.gpsimd(_gp2)

        def _v6(v):
            wt(v, st["r14"], st["consts"])
            st["r15"] = c.DV.inc(v.tensor_scalar(c.rt_ag[:, :], c.rt_ag[:, :],
                                                 c.selmk[:, 0:1], None, ALU.mult))
        blk.vector(_v6)

        def _pe5(t):
            wt(t, st["r15"])
            st["r16"] = c.PE.inc(t.matmul(c.pX[0:1, 8:16], c.ones128f[0:E, :], c.rt_ag[:, :],
                                          start=True, stop=True))
        blk.tensor(_pe5)

        def _v7(v):
            wt(v, st["r16"])
            st["r13"] = c.DV.inc(v.tensor_copy(c.rt_oh[:, :], c.pX[0:1, 8:16]))
        blk.vector(_v7)

    def _v8(v):
        wt(v, st["r13"], st["io7"])
        m = c.DV.inc(v.tensor_mul(c.rt_lgrow[:, :], c.rt_oh[:, :], c.io7mi[:, :]))
        wt(v, m)
        m = c.DV.inc(v.reduce_max(c.rt_mx[:, :], c.rt_lgrow[:, :], axis=AX.X))
        wt(v, m)
        st["r17"] = c.DV.inc(v.tensor_scalar(c.rt_mx[:, :], c.rt_mx[:, :], -1.0, 7.0,
                                             ALU.mult, ALU.add))
    blk.vector(_v8)

    def _pe6(t):
        wt(t, st["r17"])
        c.PE.inc(t.matmul(c.pX[:, 6:7], c.ones1x128[:, :], c.rt_mx[:, :], start=True, stop=True))
        st["r18"] = c.PE.inc(t.matmul(c.pX[:, 16:24], c.ones1x128[:, :], c.rt_oh[:, :],
                                      start=True, stop=True))
    blk.tensor(_pe6)

    def _v9(v):
        wt(v, st["r18"])
        m1 = c.DV.inc(v.tensor_copy(c.rt_selB[:, :], c.pX[:, 6:7]))
        m2 = c.DV.inc(v.tensor_copy(c.rt_ohB[:, :], c.pX[:, 16:24]))
        wt(v, m1, m2, st["io1"], st["io2"])
        m = c.DV.inc(v.scalar_tensor_tensor(c.idxW1f[:, :], c.rt_selB[:, 0:1].broadcast_to([128, DC]),
                                            float(D), c.ioW1[:, :], ALU.mult, ALU.add))
        m2 = c.DV.inc(v.scalar_tensor_tensor(c.idxW2f[:, :], c.rt_selB[:, 0:1].broadcast_to([128, FC]),
                                             float(FF), c.ioW2[:, :], ALU.mult, ALU.add))
        wt(v, m, m2)
        st["idx1"] = c.DV.inc(v.tensor_copy(c.idxW1[:, :], c.idxW1f[:, :]))
        st["idx2"] = c.DV.inc(v.tensor_copy(c.idxW2[:, :], c.idxW2f[:, :]))
        wt(v, st["consts"])
        m = c.DV.inc(v.tensor_scalar(c.b1sel[:, :], c.b1cm[:, 0:FC], c.rt_ohB[:, 0:1],
                                     None, ALU.mult))
        for e in range(1, E):
            wt(v, m)
            m = c.DV.inc(v.scalar_tensor_tensor(c.b1sel[:, :], c.b1cm[:, e * FC:(e + 1) * FC],
                                                c.rt_ohB[:, e:e + 1], c.b1sel[:, :],
                                                ALU.mult, ALU.add))
        st["b1sel"] = m
        m = c.DV.inc(v.tensor_scalar(c.b2sel[:, :], c.b2cm[:, 0:DC], c.rt_ohB[:, 0:1],
                                     None, ALU.mult))
        for e in range(1, E):
            wt(v, m)
            m = c.DV.inc(v.scalar_tensor_tensor(c.b2sel[:, :], c.b2cm[:, e * DC:(e + 1) * DC],
                                                c.rt_ohB[:, e:e + 1], c.b2sel[:, :],
                                                ALU.mult, ALU.add))
        wt(v, m)
        st["b2selb"] = c.DV.inc(v.tensor_copy(c.b2selb[:, :], c.b2sel[:, :]))
    blk.vector(_v9)

    def _pe7(t):
        wt(t, st["b2selb"], st["wo"])
        m = None
        for m3 in range(DC):
            for k in range(DC):
                m = c.PE.inc(t.matmul(c.pX[:, 32 + m3:33 + m3],
                                      c.wob[:, k * D + 128 * m3:k * D + 128 * (m3 + 1)],
                                      c.b2selb[:, k:k + 1], start=(k == 0), stop=(k == DC - 1)))
        st["bo_mm"] = m
    blk.tensor(_pe7)

    def _v10(v):
        wt(v, st["bo_mm"], st["consts"])
        m = c.DV.inc(v.tensor_add(c.bo15[:, :], c.pX[:, 32:32 + DC], c.bocm[:, :]))
        wt(v, m)
        st["bo15"] = c.DV.inc(v.tensor_scalar_mul(c.bo15[:, :], c.bo15[:, :], 0.15))
        st["pX_free"] = st["bo15"]
    blk.vector(_v10)


def _emit_gathers(c, blk, which):
    st = c.st

    def _gp(g):
        m = None
        for ci in range(DC + FC):
            isW1 = ci < DC
            cc = ci if isW1 else ci - DC
            width = FF if isW1 else D
            idxt = c.idxW1 if isW1 else c.idxW2
            idx_ms = st["idx1"] if isW1 else st["idx2"]
            srch = c.W1h if isW1 else c.W2h
            dst = c.W1b[:, cc * FF:(cc + 1) * FF] if isW1 else c.W2b[:, cc * D:(cc + 1) * D]
            wt(g, idx_ms)
            sem = c.GD if isW1 else c.GD2
            m = sem.inc(g.indirect_dma_start(
                dst, None, srch.ap()[:, :, :],
                bass.IndirectOffsetOnAxis(idxt[:, cc:cc + 1], axis=1)), 16)
            if ci == DC - 1:
                st["W1_all"] = m
        st["W2_all"] = m
    blk.gpsimd(_gp)


def _emit_ffn(c, blk, call, srch, dsth, do_ln2, skip_stats=False):
    st = c.st
    src_ready = st.get("gd_after_attn") if call == 2 else None
    xb, ub, hb, rb, sqs = c.xb, c.ub, c.hb, c.rb, c.sqs
    stats = c.stats1
    pH, pR, pO, pS, pX = c.pH, c.pR, c.pO, c.pS, c.pX

    def src_ap(n):
        return bass.AP(srch, n * TB, [[TSH, 128], [128 * TSH, DC], [1, TB]])

    def dst_ap(n, ch):
        return bass.AP(dsth, n * TB + ch * 128 * TSH, [[TSH, 128], [1, TB]])

    xb_last = [None, None]
    ub_last = None
    pS_prev = None
    for n in range(NB) if not skip_stats else []:
        slot = n % 2

        def _sync(s, n=n, slot=slot, war=xb_last[slot]):
            wt(s, war, src_ready)
            st[f"px{call}_{n}"] = c.LX[slot].inc(s.dma_start(
                xb[slot][:, :].rearrange("p (c t) -> p c t", c=DC), src_ap(n)), 16)
        blk.sync(_sync)

        def _sc(s, n=n, slot=slot, prev_ub=ub_last):
            wt(s, st[f"px{call}_{n}"], prev_ub)
            st[f"psq{call}_{n}"] = c.SC.inc(s.activation(ub[:, :], xb[slot][:, :], AF.Square))
        blk.scalar(_sc)

        def _vc(v, n=n, slot=slot, prev_ub=ub_last):
            wt(v, st[f"px{call}_{n}"], prev_ub)
            st[f"pxc{call}_{n}"] = c.DV.inc(v.tensor_copy(hb[:, 0:TSH], xb[slot][:, :]))
        blk.vector(_vc)

        def _pe(t, n=n, slot=slot, prev=pS_prev):
            wt(t, st[f"psq{call}_{n}"], st[f"pxc{call}_{n}"], prev)
            for k in range(DC):
                t.matmul(pS[0:1, :], c.ones128b[:, :], hb[:, k * TB:(k + 1) * TB],
                         start=(k == 0), stop=(k == DC - 1))
            mm = None
            for k in range(DC):
                mm = t.matmul(pS[32:33, :], c.ones128b[:, :], ub[:, k * TB:(k + 1) * TB],
                              start=(k == 0), stop=(k == DC - 1), tile_position=(0, 32))
            st[f"pmm{call}_{n}"] = c.PE.inc(mm)
        blk.tensor(_pe)
        xb_last[slot] = st[f"pmm{call}_{n}"]
        ub_last = st[f"pmm{call}_{n}"]

        def _v(v, n=n):
            wt(v, st[f"pmm{call}_{n}"])
            c.DV.inc(v.tensor_copy(stats[0:1, n * TB:(n + 1) * TB], pS[0:1, :]))
            st[f"pst{call}_{n}"] = c.DV.inc(
                v.tensor_copy(stats[32:33, n * TB:(n + 1) * TB], pS[32:33, :]))
        blk.vector(_v)
        pS_prev = st[f"pst{call}_{n}"]

    if not skip_stats:
        _emit_finalize(c, blk, stats, stats, st[f"pst{call}_{NB-1}"], f"f{call}")
        st[f"rmv{call}"] = st[f"f{call}_done"]
    else:
        st[f"rmv{call}"] = st["rmv_ffn2"]

    bc_free = st["pX_free"]
    lnst = [None]

    def emit_load(n):
        slot = n % 2

        def _sync(s, n=n, slot=slot, war=xb_last[slot]):
            wt(s, war, src_ready)
            st[f"mx{call}_{n}"] = c.LX[slot].inc(s.dma_start(
                xb[slot][:, :].rearrange("p (c t) -> p c t", c=DC), src_ap(n)), 16)
        blk.sync(_sync)

    def emit_bcast(n):
        nonlocal bc_free

        def _pe_bc(t, n=n, war=bc_free):
            wt(t, st[f"rmv{call}"], war)
            st[f"bc{call}_{n}"] = c.PE.inc(t.matmul(
                pX[:, :], c.selr[:, 128:256], stats[0:2, n * TB:(n + 1) * TB],
                start=True, stop=True))
        blk.tensor(_pe_bc)

        def _sc_bc(s, n=n):
            wt(s, st[f"bc{call}_{n}"], st.get(f"ub{call}_{n-1}"))
            st[f"bcE{call}_{n}"] = c.SC.inc(s.copy(c.rB_t[:, :], pX[:, :]))
        blk.scalar(_sc_bc)

        def _pe_bc2(t, n=n):
            wt(t, st[f"bcE{call}_{n}"])
            st[f"bc2{call}_{n}"] = c.PE.inc(t.matmul(
                pX[:, :], c.selr[:, 0:128], stats[0:2, n * TB:(n + 1) * TB],
                start=True, stop=True))
        blk.tensor(_pe_bc2)

        def _sc_bc2(s, n=n):
            wt(s, st[f"bc2{call}_{n}"], st.get(f"ub{call}_{n-1}"))
            st[f"bcE2{call}_{n}"] = c.SC.inc(s.copy(c.mB_t[:, :], pX[:, :]))
        blk.scalar(_sc_bc2)
        bc_free = st[f"bcE2{call}_{n}"]

    def emit_norm(n):
        slot = n % 2
        tf = c.tfp[slot]

        def _v_u(v, n=n, slot=slot, tf=tf):
            wt(v, st[f"mx{call}_{n}"], st[f"bcE{call}_{n}"], st[f"bcE2{call}_{n}"],
               st.get(f"tfwar{call}_{n-2}"), st.get(f"h{call}_{n-1}_last"),
               (st.get(f"pmm{call}_{NB-1}") if not skip_stats else None) if n <= 1 else None)
            m = None
            for ci in range(DC):
                m = c.DV.inc(v.tensor_mul(tf[:, ci * TB:(ci + 1) * TB],
                                          xb[slot][:, ci * TB:(ci + 1) * TB], c.rB_t[:, :]))
            wt(v, m)
            for ci in range(DC):
                m = c.DV.inc(v.tensor_sub(tf[:, ci * TB:(ci + 1) * TB],
                                          tf[:, ci * TB:(ci + 1) * TB], c.mB_t[:, :]))
            wt(v, m)
            st[f"ub{call}_{n}"] = c.DV.inc(v.tensor_copy(ub[:, :], tf[:, :]))
        blk.vector(_v_u)

    def emit_mm1(n):
        for m1 in range(FC):
            def _pe_h(t, n=n, m1=m1):
                if m1 == 0:
                    wt(t, st[f"ub{call}_{n}"], st["W1_all"])
                wt(t, st.get(f"hE{call}_{n}_{m1-2}") if m1 >= 2 else
                   st.get(f"hE{call}_{n-1}_{FC-2+m1}"))
                mm = None
                for k in range(DC):
                    mm = t.matmul(pH[m1 % 2][:, :],
                                  c.W1b[:, k * FF + 128 * m1:k * FF + 128 * (m1 + 1)],
                                  ub[:, k * TB:(k + 1) * TB],
                                  start=(k == 0), stop=(k == DC - 1))
                st[f"h{call}_{n}_{m1}"] = c.PE.inc(mm)
                if m1 == FC - 1:
                    st[f"h{call}_{n}_last"] = st[f"h{call}_{n}_{m1}"]
            blk.tensor(_pe_h)

            def _sc_h(s, n=n, m1=m1):
                if m1 == 0:
                    wt(s, st["b1sel"])
                    if n == 0 and not skip_stats:
                        wt(s, st[f"pmm{call}_{NB-1}"])
                wt(s, st[f"h{call}_{n}_{m1}"], st.get(f"r2l{call}_{n-1}"))
                st[f"hE{call}_{n}_{m1}"] = c.SC.inc(s.activation(
                    hb[:, m1 * TB:(m1 + 1) * TB], pH[m1 % 2][:, :],
                    AF.Gelu, bias=c.b1sel[:, m1:m1 + 1]))
            blk.scalar(_sc_h)

    def emit_mm2(n):
        for m2 in range(DC):
            def _pe_r(t, n=n, m2=m2):
                if m2 == 0:
                    wt(t, st["W2_all"], st.get(f"l2mm{call}_{n-1}"),
                       st.get(f"l2e{call}_{n-1}"))
                    for k2 in range(FC):
                        wt(t, st[f"hE{call}_{n}_{k2}"])
                wt(t, st.get(f"rE{call}_{n}_{m2-2}") if m2 >= 2 else
                   st.get(f"rE{call}_{n-1}_{DC-2+m2}"))
                mm = None
                for k2 in range(FC):
                    mm = t.matmul(pR[m2 % 2][:, :],
                                  c.W2b[:, k2 * D + 128 * m2:k2 * D + 128 * (m2 + 1)],
                                  hb[:, k2 * TB:(k2 + 1) * TB],
                                  start=(k2 == 0), stop=(k2 == FC - 1))
                st[f"r2mm{call}_{n}_{m2}"] = c.PE.inc(mm)
                if m2 == DC - 1:
                    st[f"r2l{call}_{n}"] = st[f"r2mm{call}_{n}_{m2}"]
            blk.tensor(_pe_r)

            def _sc_r(s, n=n, m2=m2):
                wt(s, st[f"r2mm{call}_{n}_{m2}"], st.get(f"o3l{call}_{n-1}"))
                st[f"rE{call}_{n}_{m2}"] = c.SC.inc(
                    s.copy(rb[:, m2 * TB:(m2 + 1) * TB], pR[m2 % 2][:, :]))
            blk.scalar(_sc_r)

    def emit_mm3(n):
        slot = n % 2
        tf = c.tfp[slot]
        for m3 in range(DC):
            def _pe_o(t, n=n, m3=m3):
                if m3 == 0:
                    wt(t, st["wo"])
                    for k3 in range(DC):
                        wt(t, st[f"rE{call}_{n}_{k3}"])
                wt(t, st.get(f"xn{call}_{n}_{m3-2}") if m3 >= 2 else
                   st.get(f"xn{call}_{n-1}_{DC-2+m3}"))
                mm = None
                for k3 in range(DC):
                    mm = t.matmul(pO[m3 % 2][:, :],
                                  c.wob[:, k3 * D + 128 * m3:k3 * D + 128 * (m3 + 1)],
                                  rb[:, k3 * TB:(k3 + 1) * TB],
                                  start=(k3 == 0), stop=(k3 == DC - 1))
                st[f"o{call}_{n}_{m3}"] = c.PE.inc(mm)
                if m3 == DC - 1:
                    st[f"o3l{call}_{n}"] = st[f"o{call}_{n}_{m3}"]
            blk.tensor(_pe_o)

            def _sc_o(s, n=n, m3=m3):
                if m3 == 0:
                    wt(s, st["bo15"])
                wt(s, st[f"o{call}_{n}_{m3}"])
                st[f"oE{call}_{n}_{m3}"] = c.SC.inc(s.activation(
                    pO[m3 % 2][:, :], pO[m3 % 2][:, :], AF.Identity,
                    bias=c.bo15[:, m3:m3 + 1], scale=0.15))
            blk.scalar(_sc_o)

            def _v_xn(v, n=n, m3=m3, slot=slot, tf=tf):
                wt(v, st[f"oE{call}_{n}_{m3}"], st[f"ub{call}_{n}"])
                m = c.DV.inc(v.scalar_tensor_tensor(
                    tf[:, m3 * TB:(m3 + 1) * TB], tf[:, m3 * TB:(m3 + 1) * TB], 0.5,
                    xb[slot][:, m3 * TB:(m3 + 1) * TB], ALU.mult, ALU.add))
                wt(v, m)
                st[f"xn{call}_{n}_{m3}"] = c.DV.inc(v.tensor_add(
                    tf[:, m3 * TB:(m3 + 1) * TB], tf[:, m3 * TB:(m3 + 1) * TB],
                    pO[m3 % 2][:, :]))
            blk.vector(_v_xn)

            def _gp_st(g, n=n, m3=m3, slot=slot, tf=tf):
                wt(g, st[f"xn{call}_{n}_{m3}"])
                st[f"tfst{call}_{n}_{m3}"] = c.GDS[slot * DC + m3].inc(
                    g.dma_start(dst_ap(n, m3), tf[:, m3 * TB:(m3 + 1) * TB]), 16)
            blk.gpsimd(_gp_st)
        st[f"stored{call}_{n}"] = st[f"tfst{call}_{n}_{DC-1}"]

        if do_ln2:
            for ci in range(DC):
                def _sc_l2(s, n=n, ci=ci):
                    wt(s, st[f"xn{call}_{n}_{ci}"],
                       st.get(f"l2q{call}_{n}_{ci-1}"),
                       lnst[0] if ci == 0 else None)
                    st[f"l2s{call}_{n}_{ci}"] = c.SC.inc(s.activation(
                        sqs[:, :], tf[:, ci * TB:(ci + 1) * TB], AF.Square))
                blk.scalar(_sc_l2)

                def _pe_l2(t, n=n, ci=ci):
                    if ci == 0:
                        wt(t, st.get(f"l2e{call}_{n-1}"), st[f"o3l{call}_{n}"])
                    wt(t, st[f"l2s{call}_{n}_{ci}"], st[f"xn{call}_{n}_{ci}"])
                    t.matmul(pS[0:1, :], c.ones128f[:, :], tf[:, ci * TB:(ci + 1) * TB],
                             start=(ci == 0), stop=(ci == DC - 1))
                    mm = t.matmul(pR[0][0:1, :], c.ones128b[:, :], sqs[:, :],
                                  start=(ci == 0), stop=(ci == DC - 1))
                    st[f"l2q{call}_{n}_{ci}"] = c.PE.inc(mm)
                blk.tensor(_pe_l2)
            st[f"l2mm{call}_{n}"] = st[f"l2q{call}_{n}_{DC-1}"]

            def _v_l2(v, n=n):
                wt(v, st[f"l2mm{call}_{n}"], st.get(f"l2st{call}_{n-1}"))
                c.DV.inc(v.tensor_copy(c.lnrow[0:1, :], pS[0:1, :]))
                st[f"l2e{call}_{n}"] = c.DV.inc(v.tensor_copy(c.lnrow[32:33, :], pR[0][0:1, :]))
            blk.vector(_v_l2)

            def _gp_l2(g, n=n):
                wt(g, st[f"l2e{call}_{n}"])
                c.GDL.inc(g.dma_start(bass.AP(c.ln2dh, n * TB, [[TSH, 1], [1, TB]]),
                                      c.lnrow[0:1, :]), 16)
                st[f"l2st{call}_{n}"] = c.GDL.inc(
                    g.dma_start(bass.AP(c.ln2dh, TSH + n * TB, [[TSH, 1], [1, TB]]),
                                c.lnrow[32:33, :]), 16)
            blk.gpsimd(_gp_l2)
            lnst[0] = st[f"l2st{call}_{n}"]
            st[f"tfwar{call}_{n}"] = [st[f"l2mm{call}_{n}"]] + \
                [st[f"tfst{call}_{n}_{m3}"] for m3 in range(DC)]
        else:
            st[f"tfwar{call}_{n}"] = [st[f"tfst{call}_{n}_{m3}"] for m3 in range(DC)]
        xb_last[n % 2] = st[f"xn{call}_{n}_{DC-1}"]

    emit_load(0)
    emit_bcast(0)
    emit_norm(0)
    for n in range(NB):
        if n + 1 < NB:
            emit_load(n + 1)
            emit_bcast(n + 1)
        emit_mm1(n)
        if n + 1 < NB:
            emit_norm(n + 1)
        emit_mm2(n)
        emit_mm3(n)
    st["ffn_done"] = st[f"stored{call}_{NB-1}"]
    st[f"gd_after_ffn{call}"] = [(s_, s_.v) for s_ in c.GDS] + [(c.GDL, c.GDL.v)]
    st["pX_free"] = bc_free
    st["xb_free"] = list(xb_last)


def _emit_finalize(c, blk, stats, dst, ms_rows, tag, rsum=0, rsq=32, dst_war=None):
    st = c.st

    def _g0(g):
        wt(g, ms_rows)
        c.GDL.inc(g.dma_start(c.fin0[:, :], stats[rsum:rsum + 1, :]), 16)
        st[f"{tag}_ld"] = c.GDL.inc(g.dma_start(c.fin1[:, :], stats[rsq:rsq + 1, :]), 16)
    blk.gpsimd(_g0)

    def _v(v):
        wt(v, st[f"{tag}_ld"])
        m = c.DV.inc(v.tensor_scalar_mul(c.fin0[:, :], c.fin0[:, :], 1.0 / D))
        m2 = c.DV.inc(v.tensor_scalar_mul(c.fin1[:, :], c.fin1[:, :], 1.0 / D))
        wt(v, m, m2)
        m = c.DV.inc(v.tensor_mul(c.fin2[:, :], c.fin0[:, :], c.fin0[:, :]))
        wt(v, m)
        m = c.DV.inc(v.tensor_sub(c.fin1[:, :], c.fin1[:, :], c.fin2[:, :]))
        wt(v, m)
        m = c.DV.inc(v.tensor_scalar_add(c.fin1[:, :], c.fin1[:, :], EPS))
        wt(v, m)
        st[f"{tag}_rv"] = c.DV.inc(v.reciprocal(c.fin1[:, :], c.fin1[:, :]))
    blk.vector(_v)

    def _s(s):
        wt(s, st[f"{tag}_rv"])
        st[f"{tag}_rs"] = c.SC.inc(s.activation(c.fin2[:, :], c.fin1[:, :], AF.Sqrt))
    blk.scalar(_s)

    def _v2(v):
        wt(v, st[f"{tag}_rs"])
        st[f"{tag}_ms"] = c.DV.inc(v.tensor_mul(c.fin3[:, :], c.fin0[:, :], c.fin2[:, :]))
    blk.vector(_v2)

    def _g1(g):
        wt(g, st[f"{tag}_ms"], dst_war)
        c.GDL.inc(g.dma_start(dst[0:1, :], c.fin3[:, :]), 16)
        st[f"{tag}_done"] = c.GDL.inc(g.dma_start(dst[1:2, :], c.fin2[:, :]), 16)
    blk.gpsimd(_g1)


def _emit_attn(c, blk, final):
    st = c.st
    xh = c.xb[0]
    rstdB = c.xb[1]
    u2f = c.tfp[0]
    u2b = c.ub
    a_t, nx_t, ml_t, P_t, y_t = c.a_t, c.nx_t, c.ml_t, c.P_t, c.y_t
    msB = c.stats1
    stats2 = c.stats2
    pH, pR, pO, pS, pX = c.pH, c.pR, c.pO, c.pS, c.pX

    def _sy0(s):
        wt(s, st["gd_after_ffn1"])
        c.LD.inc(s.dma_start(stats2[0:1, :], bass.AP(c.ln2dh, 0, [[TSH, 1], [1, TSH]])), 16)
        st["l2ld"] = c.LD.inc(s.dma_start(stats2[32:33, :],
                                          bass.AP(c.ln2dh, TSH, [[TSH, 1], [1, TSH]])), 16)
    blk.sync(_sy0)

    _emit_finalize(c, blk, stats2, stats2, st["l2ld"], "fa")
    st["rmv2"] = st["fa_done"]

    for i in range(2 * NB):
        isr = i < NB
        nb = i if isr else i - NB
        bank = pX if i % 2 == 0 else pS
        dstt = rstdB if isr else msB
        sel = c.selr[:, 128:256] if isr else c.selr[:, 0:128]

        def _pe_rb(t, i=i, nb=nb, bank=bank, sel=sel, war=st["pX_free"]):
            wt(t, st["rmv2"], war if i <= 1 else None, st.get(f"rbE{i-2}"))
            st[f"rbM{i}"] = c.PE.inc(t.matmul(
                bank[:, :], sel, stats2[0:2, nb * TB:(nb + 1) * TB],
                start=True, stop=True))
        blk.tensor(_pe_rb)

        def _sc_rb(s, i=i, nb=nb, bank=bank, dstt=dstt):
            wt(s, st[f"rbM{i}"])
            st[f"rbE{i}"] = c.SC.inc(s.copy(dstt[:, nb * TB:(nb + 1) * TB], bank[:, :]))
        blk.scalar(_sc_rb)
    st["rstdB"] = st[f"rbE{NB-1}"]
    st["msB"] = st[f"rbE{2*NB-1}"]
    st["pX_free"] = st["msB"]

    def S_load(h):
        def _sync(s, h=h):
            wt(s, st[f"tfst1_{NB-2}_{h}"], st[f"tfst1_{NB-1}_{h}"],
               st.get(f"pa{h-1}"))
            st[f"ax{h}"] = c.LX[0].inc(s.dma_start(
                xh[:, :], bass.AP(c.x1sh, h * 128 * TSH, [[TSH, 128], [1, TSH]])), 16)
        blk.sync(_sync)

    def S_u2(h):
        def _v_u1(v, h=h):
            wt(v, st[f"ax{h}"], st["rstdB"], st.get(f"nx{h-1}"))
            st[f"u2m{h}"] = c.DV.inc(v.tensor_mul(u2f[:, :], xh[:, :], rstdB[:, :]))
        blk.vector(_v_u1)

        def _v_u2(v, h=h):
            wt(v, st[f"u2m{h}"], st["msB"])
            st[f"u2w{h}"] = c.DV.inc(v.tensor_sub(u2f[:, :], u2f[:, :], msB[:, :]))
        blk.vector(_v_u2)

        for q in range(4):
            def _v_ub(v, h=h, q=q):
                wt(v, st[f"u2w{h}"],
                   st.get(f"zx{h-1}_{NB-1}") if q == 0 else None)
                st[f"u2d{h}_{q}"] = c.SC.inc(v.copy(
                    u2b[:, 2 * q * TB:(2 * q + 2) * TB],
                    u2f[:, 2 * q * TB:(2 * q + 2) * TB]))
            blk.scalar(_v_ub)
        st[f"u2d{h}"] = st[f"u2d{h}_3"]

    def S_gates(h):
        for q in range(4):
            for i in range(2):
                nb = 2 * q + i

                def _pe_g(t, h=h, nb=nb, i=i, q=q):
                    wt(t, st[f"u2d{h}_{q}"])
                    if nb == 0:
                        wt(t, st["wab"])
                    if i == 0:
                        wt(t, st.get(f"taE{h}_{q-1}") if q >= 1 else st.get(f"txE{h-1}_3"))
                        wt(t, st.get(f"txE{h}_{q-1}") if q >= 1 else None)
                    st[f"za{h}_{nb}"] = c.PE.inc(t.matmul(
                        c.pBig[:, i * TB:(i + 1) * TB],
                        c.wab[:, h * 128:(h + 1) * 128],
                        u2b[:, nb * TB:(nb + 1) * TB], start=True, stop=True))
                    st[f"zx{h}_{nb}"] = c.PE.inc(t.matmul(
                        c.pBig[:, (2 + i) * TB:(3 + i) * TB],
                        c.winb[:, h * 128:(h + 1) * 128],
                        u2b[:, nb * TB:(nb + 1) * TB], start=True, stop=True))
                blk.tensor(_pe_g)

            def _sc_g(s, h=h, q=q):
                if q == 0:
                    wt(s, st["binh"], st["bahh"], st.get(f"sP{h-1}"), st.get(f"s1{h-1}"))
                wt(s, st[f"zx{h}_{2*q+1}"])
                st[f"taE{h}_{q}"] = c.SC.inc(s.activation(
                    a_t[:, 2 * q * TB:(2 * q + 2) * TB], c.pBig[:, 0:2 * TB],
                    AF.Tanh, scale=0.5, bias=c.bahh[:, h:h + 1]))
                st[f"txE{h}_{q}"] = c.SC.inc(s.activation(
                    nx_t[:, 2 * q * TB:(2 * q + 2) * TB], c.pBig[:, 2 * TB:4 * TB],
                    AF.Tanh, scale=0.5, bias=c.binh[:, h:h + 1]))
            blk.scalar(_sc_g)

    def S_pre(h):
        def _sc_a(s, h=h):
            wt(s, st[f"taE{h}_3"], st["sp4n"], st.get(f"pt{h-1}"),
               st.get(f"nx{h-1}"), st.get(f"cci{h-1}"), st["ones"])
            m = c.SC.inc(s.activation(a_t[:, :], a_t[:, :], AF.Identity,
                                      bias=c.sp4n[:, h:h + 1], scale=c.sp4n[:, h:h + 1],
                                      accum_out=c.sumlog[:, 0:1]))
            st[f"la{h}"] = m
            wt(s, m)
            m = c.SC.inc(s.activation(ml_t[:, :], a_t[:, :], AF.Exp, scale=2.0))
            st[f"e2{h}"] = m
            wt(s, m)
            st[f"mlt{h}"] = c.SC.inc(s.activation(ml_t[:, :], ml_t[:, :], AF.Sqrt,
                                                  scale=-1.0, bias=c.ones128f[:, :1]))
            wt(s, st[f"e2{h}"])
            st[f"a{h}"] = c.SC.inc(s.activation(a_t[:, :], a_t[:, :], AF.Exp))
            wt(s, st[f"la{h}"])
            st[f"pt{h}"] = c.SC.inc(s.activation(c.summ[:, 0:1], c.sumlog[:, 0:1], AF.Exp))
        blk.scalar(_sc_a)

        def _v_rs(v, h=h):
            wt(v, st[f"mlt{h}"], st["omr0"], st["metaB"])
            st[f"m0{h}"] = c.DV.inc(v.scalar_tensor_tensor(
                ml_t[:, 0:1], ml_t[:, 0:1], c.omr0[:, 0:1], c.metaB[:, 0:1],
                ALU.mult, ALU.add))
        blk.vector(_v_rs)

        def _v_nx(v, h=h):
            wt(v, st[f"m0{h}"])
            m = c.DV.inc(v.scalar_tensor_tensor(u2f[:, :], ml_t[:, :], 0.5, u2f[:, :],
                                                ALU.mult, ALU.mult))
            wt(v, m, st[f"txE{h}_3"])
            st[f"nx{h}"] = c.DV.inc(v.scalar_tensor_tensor(nx_t[:, :], nx_t[:, :], 1.0,
                                                           u2f[:, :], ALU.add, ALU.mult))
            wt(v, st[f"a{h}"])
            st[f"a0{h}"] = c.DV.inc(v.tensor_scalar(a_t[:, 0:1], a_t[:, 0:1],
                                                    c.omr0[:, 0:1], None, ALU.mult))
        blk.vector(_v_nx)

    def S_scan(h):
        def _v_s1(v, h=h):
            wt(v, st[f"nx{h}"], st[f"a0{h}"], st.get(f"x2st{h-1}"), st.get(f"l3s{h-1}"),
               st.get(f"x2c{h-1}"), st.get(f"l3m{h-1}"), st.get(f"l3x{h-1}"))
            st[f"s1{h}"] = c.DV.inc(v.tensor_tensor_scan(y_t[:, :], a_t[:, :], nx_t[:, :],
                                                         0.0, ALU.mult, ALU.add))
            wt(v, st[f"s1{h}"], st.get(f"cci{h-1}"))
            st[f"yl{h}"] = c.DV.inc(v.tensor_copy(c.summ[:, 1:2], y_t[:, TSH - 1:TSH]))
            wt(v, st[f"yl{h}"])
            st[f"pa{h}"] = c.DV.inc(v.tensor_add(y_t[:, :], y_t[:, :], xh[:, :]))
            wt(v, st[f"nx{h}"], st.get(f"l3P{h-1}"))
            st[f"sP{h}"] = c.DV.inc(v.tensor_tensor_scan(P_t[:, :], a_t[:, :], a_t[:, :],
                                                         1.0, ALU.mult, ALU.bypass))
        blk.vector(_v_s1)

    def S_cc(h):
        def _gp_cc(g, h=h):
            wt(g, st[f"yl{h}"], st[f"pt{h}"])
            st[f"cci{h}"] = c.GDA.inc(g.dma_start(c.ccin[h].ap()[:, :], c.summ[:, :]), 16)
            wt(g, st[f"cci{h}"])
            m = c.CC.inc(g.collective_compute(
                "AllGather", ALU.bypass, replica_groups=[list(range(NCORE))],
                ins=[c.ccin[h].ap().opt()], outs=[c.ccout[h].ap().opt()]))
            wt(g, m, st.get(f"carry{h-1}"))
            mm = None
            for j in range(NCORE):
                mm = c.GDA.inc(g.dma_start(c.ag_t[:, 2 * j:2 * j + 2],
                                           c.ccout[h].ap()[128 * j:128 * (j + 1), :]), 16)
            st[f"ag{h}"] = mm
        blk.gpsimd(_gp_cc)

    def S_fold(h):
        def _v_cf(v, h=h):
            wt(v, st[f"ag{h}"], st["omask"], st["metaB"])
            m = c.DV.inc(v.tensor_mul(c.peff[:, :], _strided(c.ag_t, 0), c.metaB[:, 1:9]))
            wt(v, m)
            m = c.DV.inc(v.tensor_add(c.peff[:, :], c.peff[:, :], c.omask[:, :]))
            my = c.DV.inc(v.tensor_mul(c.yeff[:, :], _strided(c.ag_t, 1), c.metaB[:, 1:9]))
            wt(v, m, my)
            m = c.DV.inc(v.tensor_copy(c.carry[:, :], c.yeff[:, 0:1]))
            for j in range(1, 7):
                wt(v, m)
                m = c.DV.inc(v.scalar_tensor_tensor(
                    c.carry[:, :], c.carry[:, :], c.peff[:, j:j + 1],
                    c.yeff[:, j:j + 1], ALU.mult, ALU.add))
            st[f"carry{h}"] = m
        blk.vector(_v_cf)

    def S_fix(h):
        def _v_fx(v, h=h):
            wt(v, st[f"carry{h}"], st[f"sP{h}"])
            st[f"fx2{h}"] = c.DV.inc(v.scalar_tensor_tensor(
                y_t[:, :], P_t[:, :], c.carry[:, 0:1], y_t[:, :], ALU.mult, ALU.add))
            wt(v, st[f"fx2{h}"])
            st[f"x2c{h}"] = c.DV.inc(v.tensor_copy(c.x2last[:, h:h + 1],
                                                   y_t[:, TSH - 1:TSH]))
        blk.vector(_v_fx)

        def _sc_l3(s, h=h):
            wt(s, st[f"fx2{h}"])
            st[f"l3s{h}"] = c.SC.inc(s.activation(P_t[:, :], y_t[:, :], AF.Square))
        blk.scalar(_sc_l3)

        def _pe_l3x(t, h=h):
            wt(t, st[f"fx2{h}"], st["msB"])
            if h > 0:
                wt(t, st.get(f"l3x{h-1}"))
            mm = None
            for nb in range(NB):
                bank = pS if nb < 4 else pX
                row = 32 * (nb % 4)
                mm = t.matmul(bank[row:row + 1, :], c.ones128f[:, :],
                              y_t[:, nb * TB:(nb + 1) * TB],
                              start=(h == 0), stop=(h == H - 1),
                              tile_position=(0, row))
            st[f"l3x{h}"] = c.PE.inc(mm)
        blk.tensor(_pe_l3x)

        def _pe_l3q(t, h=h):
            wt(t, st[f"l3s{h}"], st["msB"])
            if h > 0:
                wt(t, st.get(f"l3P{h-1}"))
            mm = None
            for nb in range(NB):
                sl = c.pBig2[:, 0:TB] if nb < 4 else c.pBig2[:, TB:2 * TB]
                row = 32 * (nb % 4)
                mm = t.matmul(sl[row:row + 1, :], c.ones128f[:, :],
                              P_t[:, nb * TB:(nb + 1) * TB],
                              start=(h == 0), stop=(h == H - 1),
                              tile_position=(0, row))
            st[f"l3P{h}"] = c.PE.inc(mm)
        blk.tensor(_pe_l3q)
        st[f"l3m{h}"] = st[f"l3P{h}"]

        def _gp_x2(g, h=h):
            wt(g, st[f"fx2{h}"])
            st[f"x2st{h}"] = c.GDX.inc(g.dma_start(
                bass.AP(c.x2sh if not final else c.outh, h * 128 * TSH,
                        [[TSH, 128], [1, TSH]]), y_t[:, :]), 16)
        blk.gpsimd(_gp_x2)

    S_load(0)
    S_u2(0)
    S_gates(0)
    S_pre(0)
    S_scan(0)
    S_cc(0)
    for h in range(H):
        if h + 1 < H:
            S_load(h + 1)
            S_u2(h + 1)
            S_gates(h + 1)
            S_pre(h + 1)
        S_fold(h)
        S_fix(h)
        if h + 1 < H:
            S_scan(h + 1)
            S_cc(h + 1)

    def _v_l3e(v):
        wt(v, st[f"l3x{H-1}"], st[f"l3m{H-1}"], st["fa_ld"])
        m = None
        for nb in range(NB):
            bank = pS if nb < 4 else pX
            row = 32 * (nb % 4)
            m = c.DV.inc(v.tensor_copy(stats2[32:33, nb * TB:(nb + 1) * TB],
                                       bank[row:row + 1, :]))
            sl = c.pBig2[:, 0:TB] if nb < 4 else c.pBig2[:, TB:2 * TB]
            m = c.DV.inc(v.tensor_copy(stats2[64:65, nb * TB:(nb + 1) * TB],
                                       sl[row:row + 1, :]))
        st["l3xe"] = m
    blk.vector(_v_l3e)

    _emit_finalize(c, blk, stats2, c.stats1, st["l3xe"], "f3", rsum=32, rsq=64,
                   dst_war=st[f"u2w{H-1}"])
    st["rmv_ffn2"] = st["f3_done"]
    st["x2last_ms"] = st[f"x2c{H-1}"]
    st["gd_after_attn"] = [(c.GDX, c.GDX.v), (c.GDL, c.GDL.v)]
    st["xb_free"] = [st[f"x2st{H-1}"], st["msB"]]


def _strided(t, off):
    return bass.AP(t, off, [[16, 128], [2, 8]])



_NC_CACHE = {}


def _get_nc():
    if "nc" not in _NC_CACHE:
        _NC_CACHE["nc"] = build_program()
    return _NC_CACHE["nc"]


def kernel(L0_attn_x, L0_attn_w_in, L0_attn_b_in, L0_attn_w_a, L0_attn_b_a,
           L0_attn_a_param, L0_ffn_Wp1, L0_ffn_bp1, L0_ffn_Wp2, L0_ffn_bp2,
           L0_ffn_W1, L0_ffn_b1, L0_ffn_W2, L0_ffn_b2, L0_ffn_Wo, L0_ffn_bo):
    x = np.asarray(L0_attn_x, dtype=np.float32)
    xfm = np.ascontiguousarray(x.transpose(0, 2, 1))
    W1 = np.ascontiguousarray(np.asarray(L0_ffn_W1, np.float32))
    W2 = np.ascontiguousarray(np.asarray(L0_ffn_W2, np.float32))
    Wo = np.ascontiguousarray(np.asarray(L0_ffn_Wo, np.float32))
    wp1 = np.ascontiguousarray(np.asarray(L0_ffn_Wp1, np.float32))
    wp2 = np.ascontiguousarray(np.asarray(L0_ffn_Wp2, np.float32))
    bp1 = np.ascontiguousarray(np.asarray(L0_ffn_bp1, np.float32).reshape(128, 1))
    bp2 = np.ascontiguousarray(np.asarray(L0_ffn_bp2, np.float32).reshape(E, 1))
    b1cm = np.ascontiguousarray(np.asarray(L0_ffn_b1, np.float32).reshape(E, FC, 128)
                                .transpose(2, 0, 1).reshape(128, E * FC))
    b2cm = np.ascontiguousarray(np.asarray(L0_ffn_b2, np.float32).reshape(E, DC, 128)
                                .transpose(2, 0, 1).reshape(128, E * DC))
    bocm = np.ascontiguousarray(np.asarray(L0_ffn_bo, np.float32).reshape(DC, 128).T)
    w_in = np.ascontiguousarray(np.asarray(L0_attn_w_in, np.float32))
    w_a = np.ascontiguousarray(np.asarray(L0_attn_w_a, np.float32))
    bincm = np.ascontiguousarray(np.asarray(L0_attn_b_in, np.float32).T)
    bacm = np.ascontiguousarray(np.asarray(L0_attn_b_a, np.float32).T)
    aparcm = np.ascontiguousarray(np.asarray(L0_attn_a_param, np.float32).reshape(H, 128).T)

    in_maps = []
    for core in range(NCORE):
        b, q = core // QPB, core % QPB
        shard = np.ascontiguousarray(xfm[b, :, q * TSH:(q + 1) * TSH])
        lastrow = np.ascontiguousarray(x[b, -1, :].reshape(DC, 128).T)
        meta = np.zeros((1, 32), dtype=np.float32)
        meta[0, 0] = 1.0 if q == 0 else 0.0
        for j in range(8):
            meta[0, 1 + j] = 1.0 if (j // QPB == b and j % QPB < q) else 0.0
        for j in range(8):
            meta[0, 9 + j] = 1.0 if j == (b * QPB + QPB - 1) else 0.0
        selrows = np.zeros((2, 256), dtype=np.float32)
        selrows[0, 0:128] = 1.0
        selrows[1, 128:256] = 1.0
        in_maps.append({
            "x0": shard, "lastrow": lastrow, "meta": meta, "selrows": selrows,
            "W1": W1, "W2": W2, "Wo": Wo,
            "Wp1": wp1, "Wp2": wp2, "bp1": bp1, "bp2": bp2,
            "b1cm": b1cm, "b2cm": b2cm, "bocm": bocm,
            "w_in": w_in, "w_a": w_a, "bincm": bincm, "bacm": bacm,
            "aparcm": aparcm,
        })

    nc = _get_nc()
    res = run_bass_kernel_spmd(nc, in_maps, core_ids=list(range(NCORE)))
    out = np.empty((B, T, D), dtype=np.float32)
    for core in range(NCORE):
        b, q = core // QPB, core % QPB
        out[b, q * TSH:(q + 1) * TSH, :] = np.asarray(res.results[core]["y"]).T
    return out
